# version 02
# baseline (speedup 1.0000x reference)
from contextlib import ExitStack

import numpy as np
import concourse.bass as bass
import concourse.mybir as mybir

F32 = mybir.dt.float32
BF16 = mybir.dt.bfloat16
AF = mybir.ActivationFunctionType
ALU = mybir.AluOpType
AX = mybir.AxisListType.X

D = 2048
S = 2048
NH = 16
KCH = 16
E = 16
IM = 1408
ITE = IM // 128
ISH = 5632
ITS = ISH // 128
N_ACT_TILES = E * ITE + ITS
ROWS_ACT = E * IM + ISH
ROWS_ALL = ROWS_ACT + D
KC = ROWS_ALL // 128
SCALE = float(1.0 / np.sqrt(np.float32(128)))
LAM0 = 0.8
RMS_EPS = 1.1920928955078125e-07
LN_EPS = 1e-5
ABUF = 6


class _FakeInstr:
    def then_inc(self, sem, n=1):
        return self


class _FakeEngine:
    def wait_ge(self, sem, v):
        pass

    def __getattr__(self, name):
        return lambda *a, **k: _FakeInstr()


class Ctr:
    def __init__(self, sem, name):
        self.sem = sem
        self.name = name
        self.v = 0

    def inc(self, instr, n=1):
        instr.then_inc(self.sem, n)
        self.v += n
        return self.v


class R:
    def __init__(self):
        self.ev = {}
        self.mode = "dry"

    def mark(self, key, ctr):
        if self.mode == "dry":
            self.ev[key] = ctr.v
        else:
            assert self.ev[key] == ctr.v, (key, self.ev[key], ctr.v)

    def wait(self, eng, ctr, key):
        if self.mode == "real":
            eng.wait_ge(ctr.sem, self.ev[key])

    def wait_val(self, eng, ctr, val):
        if self.mode == "real":
            eng.wait_ge(ctr.sem, val)


def run_phase(nc, name, r, streams, counters):
    snap = {c.name: c.v for c in counters}
    r.mode = "dry"
    fake = _FakeEngine()
    for _, fn in streams:
        fn(fake)
    dry_end = {}
    for c in counters:
        dry_end[c.name] = c.v
        c.v = snap[c.name]
    r.mode = "real"
    with nc.Block(name) as blk:
        for eng_name, fn in streams:
            getattr(blk, eng_name)(fn)
    for c in counters:
        assert c.v == dry_end[c.name], (c.name, c.v, dry_end[c.name])


def build_program(upto="f2", dumps=()):
    nc = bass.Bass("TRN2", target_bir_lowering=False)

    xT_d = nc.dram_tensor("xT", [D, S], F32, kind="ExternalInput").ap()
    wqT_d = nc.dram_tensor("wqT", [D, D], F32, kind="ExternalInput").ap()
    wkT_d = nc.dram_tensor("wkT", [D, D], F32, kind="ExternalInput").ap()
    wvT_d = nc.dram_tensor("wvT", [D, D], F32, kind="ExternalInput").ap()
    rw_d = nc.dram_tensor("rw_l", [128, KCH * E], F32, kind="ExternalInput").ap()
    sgw_d = nc.dram_tensor("sgw_l", [128, KCH], F32, kind="ExternalInput").ap()
    wg_d = nc.dram_tensor("w_gate", [E, D, IM], F32, kind="ExternalInput").ap()
    wu_d = nc.dram_tensor("w_up", [E, D, IM], F32, kind="ExternalInput").ap()
    shg_d = nc.dram_tensor("sh_gate", [D, ISH], F32, kind="ExternalInput").ap()
    shu_d = nc.dram_tensor("sh_up", [D, ISH], F32, kind="ExternalInput").ap()
    wall_d = nc.dram_tensor("W_all", [ROWS_ALL, D], F32, kind="ExternalInput").ap()
    lq1_d = nc.dram_tensor("lq1", [128, 1], F32, kind="ExternalInput").ap()
    lq2_d = nc.dram_tensor("lq2", [128, 1], F32, kind="ExternalInput").ap()
    lk1_d = nc.dram_tensor("lk1", [128, 1], F32, kind="ExternalInput").ap()
    lk2_d = nc.dram_tensor("lk2", [128, 1], F32, kind="ExternalInput").ap()
    onesc_bf_d = nc.dram_tensor("ones_c_bf", [128, 1], BF16, kind="ExternalInput").ap()
    onesc_f_d = nc.dram_tensor("ones_c_f", [128, 1], F32, kind="ExternalInput").ap()
    onesr_f_d = nc.dram_tensor("ones_r_f", [1, 128], F32, kind="ExternalInput").ap()
    onesr_bf_d = nc.dram_tensor("ones_r_bf", [1, 128], BF16, kind="ExternalInput").ap()
    identf_d = nc.dram_tensor("identf", [128, 128], F32, kind="ExternalInput").ap()
    epsln_d = nc.dram_tensor("eps_ln", [128, 1], F32, kind="ExternalInput").ap()
    epsrms_d = nc.dram_tensor("eps_rms", [128, 1], F32, kind="ExternalInput").ap()

    outT_d = nc.dram_tensor("outT", [D, S], F32, kind="ExternalOutput").ap()
    acts_kind = "ExternalOutput" if "acts" in dumps else "Internal"
    acts_d = nc.dram_tensor("acts", [ROWS_ACT, S], BF16, kind=acts_kind).ap()
    combT_d = nc.dram_tensor("combT_dram", [E, S], BF16, kind="Internal").ap()
    wbf_d = nc.dram_tensor("W_all_bf", [ROWS_ALL, D], BF16, kind="Internal").ap()

    dump_d = {}
    for nm, shape, dt in (("xn", [128, KCH * S], BF16),
                          ("tmatT", [128, KCH * S], BF16),
                          ("combT", [E, S], BF16),
                          ("sg", [1, S], F32),
                          ("lam", [1, 1], F32)):
        if nm in dumps:
            dump_d[nm] = nc.dram_tensor("dump_" + nm, shape, dt,
                                        kind="ExternalOutput").ap()

    PP = [nc.alloc_psum_tensor(f"PP{i}", [128, 1024], F32) for i in range(4)]
    P = []
    for i in range(4):
        P.append(PP[i][:, 0:512])
        P.append(PP[i][:, 512:1024])
    onesc_bf = nc.alloc_sbuf_tensor("onesc_bf", [128, 1], BF16)
    onesc_f = nc.alloc_sbuf_tensor("onesc_f", [128, 1], F32)
    onesr_f = nc.alloc_sbuf_tensor("onesr_f", [1, 128], F32)
    onesr_bf = nc.alloc_sbuf_tensor("onesr_bf", [1, 128], BF16)
    identf = nc.alloc_sbuf_tensor("identf_sb", [128, 128], F32)
    lam_sb = nc.alloc_sbuf_tensor("lam_sb", [1, 1], F32)
    lamt_sb = nc.alloc_sbuf_tensor("lamt_sb", [1, 1], F32)
    epsln = nc.alloc_sbuf_tensor("epsln_sb", [128, 1], F32)
    epsrms = nc.alloc_sbuf_tensor("epsrms_sb", [128, 1], F32)
    e1_sb = nc.alloc_sbuf_tensor("e1_sb", [1, 1], F32)
    e2_sb = nc.alloc_sbuf_tensor("e2_sb", [1, 1], F32)
    tmatT = nc.alloc_sbuf_tensor("tmatT", [128, KCH * S], BF16)

    sem_guards = []

    def new_ctr(name):
        g = nc.semaphore(name)
        sem_guards.append(g)
        return Ctr(g.__enter__(), name)

    r = R()
    sb_stack = ExitStack()

    def sb(name, shape, dtype):
        return sb_stack.enter_context(nc.sbuf_tensor(name, shape, dtype))

    xn_stack = ExitStack()
    xn = xn_stack.enter_context(nc.sbuf_tensor("xn", [128, KCH * S], BF16))

    lxq1 = sb("lxq1", [128, 1], F32)
    lxq2 = sb("lxq2", [128, 1], F32)
    lxk1 = sb("lxk1", [128, 1], F32)
    lxk2 = sb("lxk2", [128, 1], F32)
    xbuf = [sb(f"xbuf{i}", [128, S], F32) for i in range(2)]
    xsqb = [sb(f"xsqb{i}", [128, S], BF16) for i in range(2)]
    mean_row = sb("mean_row", [1, S], F32)
    msq_row = sb("msq_row", [1, S], F32)
    mm_row = sb("mm_row", [1, S], F32)
    var_row = sb("var_row", [1, S], F32)
    std_row = sb("std_row", [1, S], F32)
    rstd_row = sb("rstd_row", [1, S], F32)
    lntmp = [sb(f"lntmp{i}", [128, 512], F32) for i in range(2)]

    c_ln_x = new_ctr("ln_x")
    c_ln_p = new_ctr("ln_p")
    c_ln_a = new_ctr("ln_a")
    c_ln_d = new_ctr("ln_d")

    def ln_sync(sy):
        for dst, srcd in ((onesc_bf, onesc_bf_d), (onesc_f, onesc_f_d),
                          (onesr_f, onesr_f_d), (onesr_bf, onesr_bf_d),
                          (identf, identf_d),
                          (epsln, epsln_d), (epsrms, epsrms_d),
                          (lxq1, lq1_d), (lxq2, lq2_d),
                          (lxk1, lk1_d), (lxk2, lk2_d)):
            c_ln_x.inc(sy.dma_start(dst[:], srcd), 16)
        r.mark("consts", c_ln_x)
        for c in range(KCH):
            if c >= 2:
                r.wait(sy, c_ln_a, ("sqr", c - 2))
                r.wait(sy, c_ln_p, ("sums", c - 2))
            c_ln_x.inc(sy.dma_start(xbuf[c % 2][:], xT_d[c*128:(c+1)*128, :]), 16)
            r.mark(("xA", c), c_ln_x)
        for c in range(KCH):
            if c < 2:
                r.wait(sy, c_ln_a, ("sqr", KCH - 1))
                r.wait(sy, c_ln_p, ("sums", KCH - 1))
            else:
                r.wait(sy, c_ln_d, ("xn", c - 2))
            c_ln_x.inc(sy.dma_start(xbuf[c % 2][:], xT_d[c*128:(c+1)*128, :]), 16)
            r.mark(("xB", c), c_ln_x)

    def ln_pe(t):
        r.wait(t, c_ln_x, "consts")
        t.matmul(P[6][0:1, 0:1], lxq1[:], lxk1[:], start=True, stop=True)
        c_ln_p.inc(t.matmul(P[7][0:1, 0:1], lxq2[:], lxk2[:], start=True, stop=True))
        r.mark("lamdots", c_ln_p)
        r.wait(t, c_ln_a, "lamexp")
        for c in range(KCH):
            r.wait(t, c_ln_x, ("xA", c))
            for j in range(4):
                t.matmul(P[j][0:1, :], onesc_f[:], xbuf[c % 2][:, j*512:(j+1)*512],
                         start=(c == 0), stop=(c == KCH - 1))
            r.wait(t, c_ln_a, ("sqr", c))
            last = None
            for j in range(4):
                last = t.matmul(P[4 + j][0:1, :], onesc_bf[:],
                                xsqb[c % 2][:, j*512:(j+1)*512],
                                start=(c == 0), stop=(c == KCH - 1))
            c_ln_p.inc(last)
            r.mark(("sums", c), c_ln_p)
        r.wait(t, c_ln_d, "rstd")
        for j in range(4):
            t.matmul(P[j][:], onesr_f[:], mean_row[0:1, j*512:(j+1)*512],
                     start=True, stop=True)
        last = None
        for j in range(4):
            last = t.matmul(P[4 + j][:], onesr_f[:], rstd_row[0:1, j*512:(j+1)*512],
                            start=True, stop=True)
        c_ln_p.inc(last)
        r.mark("bcast", c_ln_p)

    def ln_act(sc):
        r.wait(sc, c_ln_p, "lamdots")
        sc.activation(e1_sb[:], P[6][0:1, 0:1], AF.Exp)
        c_ln_a.inc(sc.activation(e2_sb[:], P[7][0:1, 0:1], AF.Exp))
        r.mark("lamexp", c_ln_a)
        for c in range(KCH):
            r.wait(sc, c_ln_x, ("xA", c))
            c_ln_a.inc(sc.activation(xsqb[c % 2][:], xbuf[c % 2][:], AF.Square))
            r.mark(("sqr", c), c_ln_a)
        r.wait(sc, c_ln_d, "var")
        c_ln_a.inc(sc.activation(std_row[:], var_row[:], AF.Sqrt, bias=epsln[0:1, 0:1]))
        r.mark("std", c_ln_a)

    def ln_dve(v):
        def edge(ins):
            c_ln_d.inc(ins)
            r.wait_val(v, c_ln_d, c_ln_d.v)
        r.wait(v, c_ln_a, "lamexp")
        edge(v.tensor_sub(lamt_sb[:], e1_sb[:], e2_sb[:]))
        c_ln_d.inc(v.tensor_single_scalar(lam_sb[:], lamt_sb[:], LAM0, ALU.add))
        r.mark("lam", c_ln_d)
        r.wait(v, c_ln_p, ("sums", KCH - 1))
        last = None
        for j in range(4):
            v.tensor_single_scalar(mean_row[0:1, j*512:(j+1)*512], P[j][0:1, :],
                                   1.0 / D, ALU.mult)
            last = v.tensor_single_scalar(msq_row[0:1, j*512:(j+1)*512],
                                          P[4 + j][0:1, :], 1.0 / D, ALU.mult)
        edge(last)
        edge(v.tensor_mul(mm_row[:], mean_row[:], mean_row[:]))
        c_ln_d.inc(v.tensor_sub(var_row[:], msq_row[:], mm_row[:]))
        r.mark("var", c_ln_d)
        r.wait(v, c_ln_a, "std")
        c_ln_d.inc(v.reciprocal(rstd_row[:], std_row[:]))
        r.mark("rstd", c_ln_d)
        r.wait(v, c_ln_p, "bcast")
        for c in range(KCH):
            r.wait(v, c_ln_x, ("xB", c))
            for j in range(4):
                edge(v.tensor_sub(lntmp[j % 2][:], xbuf[c % 2][:, j*512:(j+1)*512],
                                  P[j][:]))
                v.tensor_mul(xn[:, c*S + j*512: c*S + (j+1)*512], lntmp[j % 2][:],
                             P[4 + j][:])
            c_ln_d.inc(v.engine_nop())
            r.mark(("xn", c), c_ln_d)

    run_phase(nc, "ln", r, [("sync", ln_sync), ("tensor", ln_pe),
                            ("scalar", ln_act), ("vector", ln_dve)],
              [c_ln_x, c_ln_p, c_ln_a, c_ln_d])
    sb_stack.close()
    sb_stack = ExitStack()

    if upto == "ln":
        _finish_dev(nc, dump_d, dumps, xn, tmatT, None, lam_sb, new_ctr)
        _close(sem_guards)
        return nc

    qT = sb("qT", [128, S], BF16)
    kT = sb("kT", [128, S], BF16)
    vh = sb("vh", [128, S], BF16)
    vTf = sb("vTf", [128, S], F32)
    wqkv = [[sb(f"wqkv{i}_{j}", [128, S], BF16) for j in range(3)] for i in range(2)]
    expb = [sb(f"expb{i}", [128, 1024], BF16) for i in range(2)]
    r1_sb = sb("r1_sb", [1, 512], F32)
    r2l_sb = sb("r2l_sb", [1, 512], F32)
    r2t_sb = sb("r2t_sb", [1, 512], F32)
    b12_sb = sb("b12_sb", [128, 1024], F32)
    t1_sb = sb("t1_sb", [128, 512], F32)
    t2_sb = sb("t2_sb", [128, 512], F32)
    o_sb = sb("o_sb", [128, 512], F32)
    sq_sb = sb("sq_sb", [128, 512], BF16)
    srt_sb = sb("srt_sb", [128, 4], F32)
    rc4_sb = sb("rc4_sb", [128, 4], F32)
    rstd4 = sb("rstd4", [128, 4], F32)

    c_aw = new_ctr("att_w")
    c_prj = new_ctr("att_prj")
    c_pev = new_ctr("att_pev")
    c_sc = new_ctr("att_sc")
    c_exp = new_ctr("att_exp")
    c_av = new_ctr("att_av")
    c_epq = new_ctr("att_epq")
    c_epa = new_ctr("att_epa")
    c_epd = new_ctr("att_epd")

    def att_gp(gp):
        for h in range(NH):
            if h >= 2:
                r.wait(gp, c_prj, ("proj", h - 2, 2))
            for j, wsrc in enumerate((wqT_d, wkT_d, wvT_d)):
                src = wsrc.rearrange("(c p) j -> p c j", p=128)[:, :, h*128:(h+1)*128]
                dst = wqkv[h % 2][j][:].rearrange("p (c j) -> p c j", j=128)
                c_aw.inc(gp.dma_start(dst[:, 0:8, :], src[:, 0:8, :]), 16)
                c_aw.inc(gp.dma_start(dst[:, 8:16, :], src[:, 8:16, :]), 16)
            r.mark(("w", h), c_aw)

    def _att_av(t, h, i, kt):
        par = kt % 2
        r.wait(t, c_exp, ("exp", i, kt))
        if kt == 0:
            r.wait(t, c_pev, ("vtB", h))
            if i > 0:
                r.wait(t, c_epa, ("sqrt", i - 1))
                r.wait(t, c_epd, ("epdone", i - 1))
        vsl = vh[:, kt*128:(kt+1)*128]
        t.matmul(P[4][:], vsl, expb[par][:, 0:512],
                 start=(kt == 0), stop=(kt == KCH - 1))
        t.matmul(P[5][:], vsl, expb[par][:, 512:1024],
                 start=(kt == 0), stop=(kt == KCH - 1))
        t.matmul(P[6][0:1, :], onesc_bf[:], expb[par][:, 0:512],
                 start=(kt == 0), stop=(kt == KCH - 1))
        c_av.inc(t.matmul(P[7][0:1, :], onesc_bf[:], expb[par][:, 512:1024],
                          start=(kt == 0), stop=(kt == KCH - 1)))
        r.mark(("av", i, kt), c_av)

    def att_pe(t):
        for h in range(NH):
            r.wait(t, c_aw, ("w", h))
            if h > 0:
                r.wait(t, c_epa, ("sqrt", h * 4 - 1))
                r.wait(t, c_epd, ("epdone", h * 4 - 1))
            for w_i, banks in ((0, (0, 1, 2, 3)), (1, (4, 5, 6, 7))):
                last = None
                for k in range(KCH):
                    wsl = wqkv[h % 2][w_i][:, k*128:(k+1)*128]
                    for j in range(4):
                        last = t.matmul(P[banks[j]][:], wsl,
                                        xn[:, k*S + j*512: k*S + (j+1)*512],
                                        start=(k == 0), stop=(k == KCH - 1))
                c_prj.inc(last)
                r.mark(("proj", h, w_i), c_prj)
            r.wait(t, c_pev, ("evq", h))
            last = None
            for k in range(KCH):
                wsl = wqkv[h % 2][2][:, k*128:(k+1)*128]
                for j in range(4):
                    last = t.matmul(P[j][:], wsl,
                                    xn[:, k*S + j*512: k*S + (j+1)*512],
                                    start=(k == 0), stop=(k == KCH - 1))
            c_prj.inc(last)
            r.mark(("proj", h, 2), c_prj)
            r.wait(t, c_pev, ("evk", h))
            r.wait(t, c_pev, ("evv", h))
            for g in range(2):
                if g == 1:
                    r.wait(t, c_pev, ("vtA", h))
                last = None
                for kt in range(8):
                    ktg = g * 8 + kt
                    dstp = PP[2 + kt // 4][:, (kt % 4)*128:(kt % 4 + 1)*128]
                    last = t.transpose(dstp, vTf[:, ktg*128:(ktg+1)*128], identf[:])
                c_prj.inc(last)
                r.mark(("vt", h, g), c_prj)
            for qc in range(4):
                i = h * 4 + qc
                for kt in range(KCH):
                    par = kt % 2
                    spair = PP[par]
                    if kt >= 2:
                        r.wait(t, c_exp, ("exp", i, kt - 2))
                    elif i > 0:
                        r.wait(t, c_exp, ("exp", i - 1, 14 + kt))
                    t.matmul(spair[:, 0:512], kT[0:64, kt*128:(kt+1)*128],
                             qT[0:64, qc*512:(qc+1)*512],
                             start=True, stop=True, tile_position=(0, 0))
                    c_sc.inc(t.matmul(spair[:, 512:1024],
                                      kT[64:128, kt*128:(kt+1)*128],
                                      qT[64:128, qc*512:(qc+1)*512],
                                      start=True, stop=True, tile_position=(64, 0)))
                    r.mark(("sc", i, kt), c_sc)
                    if kt > 0:
                        _att_av(t, h, i, kt - 1)
                _att_av(t, h, i, KCH - 1)
                r.wait(t, c_epd, ("recips", i))
                t.matmul(P[6][:], onesr_f[:], r1_sb[:], start=True, stop=True)
                c_epq.inc(t.matmul(P[7][:], onesr_f[:], r2l_sb[:],
                                   start=True, stop=True))
                r.mark(("bcq", i), c_epq)
                r.wait(t, c_epa, ("sq", i))
                last = None
                for j in range(4):
                    last = t.matmul(P[6][:, j:j+1], sq_sb[:, j*128:(j+1)*128],
                                    onesc_bf[:], start=True, stop=True)
                c_epq.inc(last)
                r.mark(("msum", i), c_epq)
                r.wait(t, c_epd, ("osb", i))
                for j in range(4):
                    dstp = PP[2][:, j*128:(j+1)*128]
                    c_epq.inc(t.transpose(dstp, o_sb[:, j*128:(j+1)*128], identf[:]))
                    r.mark(("tr", i, j), c_epq)

    def att_act(sc):
        for h in range(NH):
            for qc in range(4):
                i = h * 4 + qc
                for kt in range(KCH):
                    par = kt % 2
                    r.wait(sc, c_sc, ("sc", i, kt))
                    if kt >= 2:
                        r.wait(sc, c_av, ("av", i, kt - 2))
                    elif kt == 0 and i > 0:
                        r.wait(sc, c_av, ("av", i - 1, KCH - 1))
                    c_exp.inc(sc.activation(expb[par][:], PP[par][:, 0:1024], AF.Exp,
                                            scale=SCALE))
                    r.mark(("exp", i, kt), c_exp)
                r.wait(sc, c_epq, ("bcq", i))
                if i > 0:
                    r.wait(sc, c_epd, ("tops", i - 1))
                c_epa.inc(sc.copy(b12_sb[:], PP[3][:, 0:1024]))
                r.mark(("bcp", i), c_epa)
                r.wait(sc, c_epd, ("osb", i))
                c_epa.inc(sc.activation(sq_sb[:], o_sb[:], AF.Square))
                r.mark(("sq", i), c_epa)
                r.wait(sc, c_epq, ("msum", i))
                c_epa.inc(sc.activation(srt_sb[:], P[6][:, 0:4], AF.Sqrt,
                                        scale=1.0 / 128, bias=epsrms[:]))
                r.mark(("sqrt", i), c_epa)

    def att_dve(v):
        for h in range(NH):
            r.wait(v, c_prj, ("proj", h, 0))
            if h > 0:
                r.wait(v, c_sc, ("sc", h * 4 - 1, KCH - 1))
            v.tensor_copy(qT[:, 0:1024], PP[0][:, 0:1024])
            c_pev.inc(v.tensor_copy(qT[:, 1024:2048], PP[1][:, 0:1024]))
            r.mark(("evq", h), c_pev)
            r.wait(v, c_prj, ("proj", h, 1))
            v.tensor_copy(kT[:, 0:1024], PP[2][:, 0:1024])
            c_pev.inc(v.tensor_copy(kT[:, 1024:2048], PP[3][:, 0:1024]))
            r.mark(("evk", h), c_pev)
            r.wait(v, c_prj, ("proj", h, 2))
            v.tensor_copy(vTf[:, 0:1024], PP[0][:, 0:1024])
            c_pev.inc(v.tensor_copy(vTf[:, 1024:2048], PP[1][:, 0:1024]))
            r.mark(("evv", h), c_pev)
            for g in range(2):
                r.wait(v, c_prj, ("vt", h, g))
                if g == 0 and h > 0:
                    r.wait(v, c_av, ("av", h * 4 - 1, KCH - 1))
                last = None
                for kt in range(8):
                    ktg = g * 8 + kt
                    srcp = PP[2 + kt // 4][:, (kt % 4)*128:(kt % 4 + 1)*128]
                    last = v.tensor_copy(vh[:, ktg*128:(ktg+1)*128], srcp)
                c_pev.inc(last)
                r.mark(("vtA" if g == 0 else "vtB", h), c_pev)
            for qc in range(4):
                i = h * 4 + qc
                r.wait(v, c_av, ("av", i, KCH - 1))
                v.reciprocal(r1_sb[:], P[6][0:1, :])
                c_epd.inc(v.reciprocal(r2t_sb[:], P[7][0:1, :]))
                r.wait_val(v, c_epd, c_epd.v)
                c_epd.inc(v.tensor_single_scalar(r2l_sb[:], r2t_sb[:], lam_sb[:],
                                                 ALU.mult))
                r.mark(("recips", i), c_epd)
                r.wait(v, c_epa, ("bcp", i))
                v.tensor_mul(t1_sb[:], P[4][:], b12_sb[:, 0:512])
                c_epd.inc(v.tensor_mul(t2_sb[:], P[5][:], b12_sb[:, 512:1024]))
                r.wait_val(v, c_epd, c_epd.v)
                c_epd.inc(v.tensor_sub(o_sb[:], t1_sb[:], t2_sb[:]))
                r.mark(("tops", i), c_epd)
                r.mark(("osb", i), c_epd)
                r.wait(v, c_epa, ("sqrt", i))
                c_epd.inc(v.reciprocal(rc4_sb[:], srt_sb[:]))
                r.wait_val(v, c_epd, c_epd.v)
                c_epd.inc(v.tensor_single_scalar(rstd4[:], rc4_sb[:], 0.2, ALU.mult))
                r.mark(("rstd", i), c_epd)
                r.wait(v, c_epd, ("rstd", i))
                for j in range(4):
                    r.wait(v, c_epq, ("tr", i, j))
                    qt = qc * 4 + j
                    dst = bass.AP(tmatT, qt * S + h, [[KCH * S, 128], [16, 128]])
                    srcp = PP[2][:, j*128:(j+1)*128]
                    c_epd.inc(v.tensor_scalar(dst, srcp, rstd4[:, j:j+1], None,
                                              ALU.mult))
                r.mark(("epdone", i), c_epd)

    run_phase(nc, "att", r, [("gpsimd", att_gp), ("tensor", att_pe),
                             ("scalar", att_act), ("vector", att_dve)],
              [c_aw, c_prj, c_pev, c_sc, c_exp, c_av, c_epq, c_epa, c_epd])
    sb_stack.close()
    sb_stack = ExitStack()

    if upto == "att":
        _finish_dev(nc, dump_d, dumps, xn, tmatT, None, lam_sb, new_ctr)
        _close(sem_guards)
        return nc

    rw_sb = sb("rw_sb", [128, KCH * E], BF16)
    sgw_sb = sb("sgw_sb", [128, KCH], BF16)
    p_sb = [sb(f"p_sb{i}", [128, E], F32) for i in range(2)]
    pw_sb = [sb(f"pw_sb{i}", [128, E], F32) for i in range(2)]
    nmax_sb = sb("nmax_sb", [128, 1], F32)
    psum_sb = sb("psum_sb", [128, 1], F32)
    rs_sb = sb("rs_sb", [128, 1], F32)
    mx_sb = sb("mx_sb", [128, 1], F32)
    mask_sb = sb("mask_sb", [128, E], F32)
    combv_sb = sb("combv_sb", [128, E], F32)
    ts_sb = sb("ts_sb", [128, 1], F32)
    ts2_sb = sb("ts2_sb", [128, 1], F32)
    probs_sb = sb("probs_sb", [128, E], F32)
    comb_sb = [sb(f"comb_sb{i}", [128, E], F32) for i in range(2)]
    combT_sb = sb("combT_sb", [E, S], BF16)
    crow_sb = [sb(f"crow_sb{i}", [1, S], BF16) for i in range(2)]
    sg_row = sb("sg_row", [1, S], F32)
    wgub = [[sb(f"wgub{i}_{j}", [128, KCH * 128], BF16) for j in range(2)]
            for i in range(2)]
    shwb = [[sb(f"shwb{i}_{j}", [128, KCH * 128], BF16) for j in range(2)]
            for i in range(2)]
    silu_sb = [sb(f"silu_sb{i}", [128, 1024], BF16) for i in range(2)]
    tmp_sb = [sb(f"tmp_sb{i}", [128, 1024], BF16) for i in range(2)]
    act_sb = [sb(f"act_sb{i}", [128, S], BF16) for i in range(2)]
    cb_sb = sb("cb_sb", [128, S], BF16)
    sgb_sb = sb("sgb_sb", [128, S], BF16)

    c_fw = new_ctr("f1_w")
    c_fp = new_ctr("f1_pe")
    c_fa = new_ctr("f1_act")
    c_fd = new_ctr("f1_dve")
    c_ao = new_ctr("f1_aout")
    c_ct = new_ctr("f1_combT")
    c_cr = new_ctr("f1_crow")
    c_wc = new_ctr("f1_wcast")

    NTT = 16

    def prev_tile(e, it):
        return (e, it - 1) if it > 0 else (e - 1, ITE - 1)

    def _crow(gp, e):
        if e == 0:
            r.wait(gp, c_ct, "bounce")
        if e >= 2:
            r.wait(gp, c_fp, ("cbq", e - 2))
        c_cr.inc(gp.dma_start(crow_sb[e % 2][:], combT_d[e:e+1, :]), 16)
        r.mark(("crow", e), c_cr)

    def f1_gp(gp):
        c_fw.inc(gp.dma_start(rw_sb[:], rw_d), 16)
        c_fw.inc(gp.dma_start(sgw_sb[:], sgw_d), 16)
        r.mark("rw", c_fw)
        _crow(gp, 0)
        _crow(gp, 1)
        for e in range(E):
            for it in range(ITE):
                m = e * ITE + it
                if m >= 2:
                    ep, ip = divmod(m - 2, ITE)
                    r.wait(gp, c_fp, ("gup", ep, ip, 1))
                for j, wsrc in enumerate((wg_d, wu_d)):
                    src = wsrc[e].rearrange("(c p) i -> p c i",
                                            p=128)[:, :, it*128:(it+1)*128]
                    dst = wgub[m % 2][j][:].rearrange("p (c i) -> p c i", i=128)
                    c_fw.inc(gp.dma_start(dst[:, 0:8, :], src[:, 0:8, :]), 16)
                    c_fw.inc(gp.dma_start(dst[:, 8:16, :], src[:, 8:16, :]), 16)
                r.mark(("wet", e, it), c_fw)
            if e + 2 < E:
                _crow(gp, e + 2)
            for k in range(e * 15, min((e + 1) * 15, KC)):
                c_wc.inc(gp.dma_start(wbf_d[k*128:(k+1)*128, :],
                                      wall_d[k*128:(k+1)*128, :]), 16)
        for it in range(ITS):
            if it >= 2:
                r.wait(gp, c_fp, ("gup_sh", it - 2, 1))
            for j, wsrc in enumerate((shg_d, shu_d)):
                src = wsrc.rearrange("(c p) i -> p c i", p=128)[:, :, it*128:(it+1)*128]
                dst = shwb[it % 2][j][:].rearrange("p (c i) -> p c i", i=128)
                c_fw.inc(gp.dma_start(dst, src), 16)
            r.mark(("wsh", it), c_fw)

    def f1_pe(t):
        r.wait(t, c_fw, "rw")
        last = None
        for k in range(KCH):
            for j in range(4):
                last = t.matmul(P[4 + j][0:1, :], sgw_sb[:, k:k+1],
                                xn[:, k*S + j*512: k*S + (j+1)*512],
                                start=(k == 0), stop=(k == KCH - 1))
        c_fp.inc(last)
        r.mark("sglog", c_fp)
        for tt in range(NTT):
            if tt >= 2:
                r.wait(t, c_fd, ("nmax", tt - 2))
            last = None
            for k in range(KCH):
                last = t.matmul(P[tt % 2][:, 0:E],
                                xn[:, k*S + tt*128: k*S + (tt+1)*128],
                                rw_sb[:, k*E:(k+1)*E],
                                start=(k == 0), stop=(k == KCH - 1))
            c_fp.inc(last)
            r.mark(("rlog", tt), c_fp)
            if tt >= 2:
                r.wait(t, c_fa, ("combT", tt - 2))
            r.wait(t, c_fd, ("comb", tt))
            c_fp.inc(t.transpose(P[2 + tt % 2][0:E, 0:128], comb_sb[tt % 2][:],
                                 identf[:]))
            r.mark(("combtr", tt), c_fp)
        r.wait(t, c_fa, "sgrow")
        last = None
        for j in range(4):
            last = t.matmul(P[4 + j][:], onesr_f[:], sg_row[0:1, j*512:(j+1)*512],
                            start=True, stop=True)
        c_fp.inc(last)
        r.mark("sgb", c_fp)
        for e in range(E):
            r.wait(t, c_cr, ("crow", e))
            if e == 0:
                r.wait(t, c_fa, "sgbe")
                r.wait(t, c_fa, ("combT", NTT - 1))
                r.wait(t, c_fd, ("nmax", NTT - 1))
            else:
                r.wait(t, c_fa, ("cbe", e - 1))
                r.wait(t, c_fa, ("silu", e - 1, ITE - 1, 0))
                r.wait(t, c_fd, ("actq", e - 1, ITE - 1, 1))
            last = None
            for j in range(4):
                last = t.matmul(P[j][:], onesr_bf[:],
                                crow_sb[e % 2][0:1, j*512:(j+1)*512],
                                start=True, stop=True)
            c_fp.inc(last)
            r.mark(("cbq", e), c_fp)
            r.wait(t, c_fa, ("cbe", e))
            for it in range(ITE):
                m = e * ITE + it
                r.wait(t, c_fw, ("wet", e, it))
                for th in range(2):
                    gA, gB = P[4*th], P[4*th + 1]
                    uA, uB = P[4*th + 2], P[4*th + 3]
                    if not (e == 0 and it == 0):
                        pe_, pit = prev_tile(e, it)
                        r.wait(t, c_fa, ("silu", pe_, pit, th))
                        r.wait(t, c_fd, ("umul", pe_, pit, th))
                    last = None
                    for k in range(KCH):
                        wsl = wgub[m % 2][0][:, k*128:(k+1)*128]
                        t.matmul(gA[:], wsl, xn[:, k*S + th*1024: k*S + th*1024 + 512],
                                 start=(k == 0), stop=(k == KCH - 1))
                        last = t.matmul(gB[:], wsl,
                                        xn[:, k*S + th*1024 + 512: k*S + (th+1)*1024],
                                        start=(k == 0), stop=(k == KCH - 1))
                    c_fp.inc(last)
                    r.mark(("g", e, it, th), c_fp)
                    last = None
                    for k in range(KCH):
                        wsl = wgub[m % 2][1][:, k*128:(k+1)*128]
                        t.matmul(uA[:], wsl, xn[:, k*S + th*1024: k*S + th*1024 + 512],
                                 start=(k == 0), stop=(k == KCH - 1))
                        last = t.matmul(uB[:], wsl,
                                        xn[:, k*S + th*1024 + 512: k*S + (th+1)*1024],
                                        start=(k == 0), stop=(k == KCH - 1))
                    c_fp.inc(last)
                    r.mark(("gup", e, it, th), c_fp)
        for it in range(ITS):
            r.wait(t, c_fw, ("wsh", it))
            for th in range(2):
                gA, gB = P[4*th], P[4*th + 1]
                uA, uB = P[4*th + 2], P[4*th + 3]
                if it == 0:
                    r.wait(t, c_fa, ("silu", E - 1, ITE - 1, th))
                    r.wait(t, c_fd, ("umul", E - 1, ITE - 1, th))
                else:
                    r.wait(t, c_fa, ("silu_sh", it - 1, th))
                    r.wait(t, c_fd, ("umul_sh", it - 1, th))
                last = None
                for k in range(KCH):
                    wsl = shwb[it % 2][0][:, k*128:(k+1)*128]
                    t.matmul(gA[:], wsl, xn[:, k*S + th*1024: k*S + th*1024 + 512],
                             start=(k == 0), stop=(k == KCH - 1))
                    last = t.matmul(gB[:], wsl,
                                    xn[:, k*S + th*1024 + 512: k*S + (th+1)*1024],
                                    start=(k == 0), stop=(k == KCH - 1))
                c_fp.inc(last)
                r.mark(("g_sh", it, th), c_fp)
                last = None
                for k in range(KCH):
                    wsl = shwb[it % 2][1][:, k*128:(k+1)*128]
                    t.matmul(uA[:], wsl, xn[:, k*S + th*1024: k*S + th*1024 + 512],
                             start=(k == 0), stop=(k == KCH - 1))
                    last = t.matmul(uB[:], wsl,
                                    xn[:, k*S + th*1024 + 512: k*S + (th+1)*1024],
                                    start=(k == 0), stop=(k == KCH - 1))
                c_fp.inc(last)
                r.mark(("gup_sh", it, th), c_fp)

    def f1_act(sc):
        r.wait(sc, c_fp, "sglog")
        last = None
        for j in range(4):
            last = sc.activation(sg_row[0:1, j*512:(j+1)*512], P[4 + j][0:1, :],
                                 AF.Sigmoid)
        c_fa.inc(last)
        r.mark("sgrow", c_fa)
        for tt in range(NTT):
            r.wait(sc, c_fp, ("rlog", tt))
            r.wait(sc, c_fd, ("nmax", tt))
            c_fa.inc(sc.activation(p_sb[tt % 2][:], P[tt % 2][:, 0:E], AF.Exp,
                                   bias=nmax_sb[:], accum_out=psum_sb[:]))
            r.mark(("rexp", tt), c_fa)
            r.wait(sc, c_fp, ("combtr", tt))
            c_fa.inc(sc.copy(combT_sb[:, tt*128:(tt+1)*128],
                             P[2 + tt % 2][0:E, 0:128]))
            r.mark(("combT", tt), c_fa)
        r.wait(sc, c_fp, "sgb")
        last = None
        for j in range(4):
            last = sc.copy(sgb_sb[:, j*512:(j+1)*512], P[4 + j][:])
        c_fa.inc(last)
        r.mark("sgbe", c_fa)
        for e in range(E):
            r.wait(sc, c_fp, ("cbq", e))
            last = None
            for j in range(4):
                last = sc.copy(cb_sb[:, j*512:(j+1)*512], P[j][:])
            c_fa.inc(last)
            r.mark(("cbe", e), c_fa)
            for it in range(ITE):
                for th in range(2):
                    gA, gB = P[4*th], P[4*th + 1]
                    r.wait(sc, c_fp, ("g", e, it, th))
                    if not (e == 0 and it == 0):
                        pe_, pit = prev_tile(e, it)
                        r.wait(sc, c_fd, ("umul", pe_, pit, th))
                    sc.activation(silu_sb[th][:, 0:512], gA[:], AF.Silu)
                    c_fa.inc(sc.activation(silu_sb[th][:, 512:1024], gB[:], AF.Silu))
                    r.mark(("silu", e, it, th), c_fa)
        for it in range(ITS):
            for th in range(2):
                gA, gB = P[4*th], P[4*th + 1]
                r.wait(sc, c_fp, ("g_sh", it, th))
                if it == 0:
                    r.wait(sc, c_fd, ("umul", E - 1, ITE - 1, th))
                else:
                    r.wait(sc, c_fd, ("umul_sh", it - 1, th))
                sc.activation(silu_sb[th][:, 0:512], gA[:], AF.Silu)
                c_fa.inc(sc.activation(silu_sb[th][:, 512:1024], gB[:], AF.Silu))
                r.mark(("silu_sh", it, th), c_fa)

    def f1_dve(v):
        for tt in range(NTT):
            r.wait(v, c_fp, ("rlog", tt))
            c_fd.inc(v.reduce_max(nmax_sb[:], P[tt % 2][:, 0:E], axis=AX, negate=True))
            r.mark(("nmax", tt), c_fd)
            r.wait(v, c_fa, ("rexp", tt))
            def edge(ins):
                c_fd.inc(ins)
                r.wait_val(v, c_fd, c_fd.v)
            edge(v.reciprocal(rs_sb[:], psum_sb[:]))
            edge(v.tensor_scalar(probs_sb[:], p_sb[tt % 2][:], rs_sb[:], None,
                                 ALU.mult))
            edge(v.tensor_copy(pw_sb[0][:], probs_sb[:]))
            for rnd in range(4):
                edge(v.reduce_max(mx_sb[:], pw_sb[rnd % 2][:], axis=AX))
                if rnd < 3:
                    edge(v.tensor_single_scalar(mask_sb[:], pw_sb[rnd % 2][:],
                                                mx_sb[:], ALU.is_ge))
                    edge(v.scalar_tensor_tensor(pw_sb[(rnd + 1) % 2][:], mask_sb[:],
                                                -1e30, pw_sb[rnd % 2][:],
                                                ALU.mult, ALU.add))
            edge(v.tensor_single_scalar(mask_sb[:], probs_sb[:], mx_sb[:], ALU.is_ge))
            edge(v.tensor_mul(combv_sb[:], probs_sb[:], mask_sb[:]))
            edge(v.reduce_sum(ts_sb[:], combv_sb[:], axis=AX))
            edge(v.tensor_scalar_max(ts2_sb[:], ts_sb[:], 1e-9))
            edge(v.reciprocal(rs_sb[:], ts2_sb[:]))
            if tt >= 2:
                r.wait(v, c_fp, ("combtr", tt - 2))
            c_fd.inc(v.tensor_scalar(comb_sb[tt % 2][:], combv_sb[:], rs_sb[:], None,
                                     ALU.mult))
            r.mark(("comb", tt), c_fd)
        for e in range(E):
            for it in range(ITE):
                n = e * ITE + it
                for th in range(2):
                    uA, uB = P[4*th + 2], P[4*th + 3]
                    r.wait(v, c_fa, ("silu", e, it, th))
                    r.wait(v, c_fp, ("gup", e, it, th))
                    if th == 0 and n >= 2:
                        r.wait_val(v, c_ao, 16 * (n - 1))
                    v.tensor_mul(tmp_sb[th][:, 0:512], silu_sb[th][:, 0:512], uA[:])
                    c_fd.inc(v.tensor_mul(tmp_sb[th][:, 512:1024],
                                          silu_sb[th][:, 512:1024], uB[:]))
                    r.mark(("umul", e, it, th), c_fd)
                    r.wait_val(v, c_fd, c_fd.v)
                    c_fd.inc(v.tensor_mul(act_sb[n % 2][:, th*1024:(th+1)*1024],
                                          tmp_sb[th][:],
                                          cb_sb[:, th*1024:(th+1)*1024]))
                    r.mark(("actq", e, it, th), c_fd)
                r.mark(("acttile", n), c_fd)
        for it in range(ITS):
            n = E * ITE + it
            for th in range(2):
                uA, uB = P[4*th + 2], P[4*th + 3]
                r.wait(v, c_fa, ("silu_sh", it, th))
                r.wait(v, c_fp, ("gup_sh", it, th))
                if th == 0 and n >= 2:
                    r.wait_val(v, c_ao, 16 * (n - 1))
                v.tensor_mul(tmp_sb[th][:, 0:512], silu_sb[th][:, 0:512], uA[:])
                c_fd.inc(v.tensor_mul(tmp_sb[th][:, 512:1024],
                                      silu_sb[th][:, 512:1024], uB[:]))
                r.mark(("umul_sh", it, th), c_fd)
                r.wait_val(v, c_fd, c_fd.v)
                c_fd.inc(v.tensor_mul(act_sb[n % 2][:, th*1024:(th+1)*1024],
                                      tmp_sb[th][:], sgb_sb[:, th*1024:(th+1)*1024]))
                r.mark(("actq_sh", it, th), c_fd)
            r.mark(("acttile", n), c_fd)

    def f1_sync(sy):
        r.wait(sy, c_fa, ("combT", NTT - 1))
        c_ct.inc(sy.dma_start(combT_d, combT_sb[:]), 16)
        r.mark("bounce", c_ct)
        for e in range(E):
            for it in range(ITE):
                n = e * ITE + it
                r.wait(sy, c_fd, ("acttile", n))
                row = e * IM + it * 128
                c_ao.inc(sy.dma_start(acts_d[row:row+128, :], act_sb[n % 2][:]), 16)
        for it in range(ITS):
            n = E * ITE + it
            r.wait(sy, c_fd, ("acttile", n))
            row = E * IM + it * 128
            c_ao.inc(sy.dma_start(acts_d[row:row+128, :], act_sb[n % 2][:]), 16)
        r.mark("all_acts", c_ao)

    run_phase(nc, "f1", r, [("gpsimd", f1_gp), ("tensor", f1_pe),
                            ("scalar", f1_act), ("vector", f1_dve),
                            ("sync", f1_sync)],
              [c_fw, c_fp, c_fa, c_fd, c_ao, c_ct, c_cr, c_wc])
    if upto == "f1":
        _finish_dev(nc, dump_d, dumps, xn, tmatT, combT_sb, lam_sb, new_ctr,
                    sg_row=sg_row)
        _close(sem_guards)
        return nc
    sb_stack.close()
    sb_stack = ExitStack()
    xn_stack.close()

    abuf = [sb(f"abuf{i}", [128, 1024], BF16) for i in range(ABUF)]
    wbuf = [sb(f"wbuf{i}", [128, 512], BF16) for i in range(ABUF)]
    xres = [[sb(f"xres{i}_{j}", [128, 1024], F32) for j in range(4)] for i in range(2)]
    osb = [[sb(f"osb{i}_{j}", [128, 1024], F32) for j in range(4)] for i in range(2)]

    c_ai = new_ctr("f2_ain")
    c_wi = new_ctr("f2_win")
    c_ri = new_ctr("f2_res")
    c_2p = new_ctr("f2_pe")
    c_2d = new_ctr("f2_dve")
    c_od = new_ctr("f2_out")

    SWEEPS = [(dmq, th) for dmq in range(4) for th in range(2)]

    def f2_sync(sy):
        r.wait(sy, c_ao, "all_acts")
        for si, (dmq, th) in enumerate(SWEEPS):
            for dm in range(4):
                if si >= 2:
                    r.wait(sy, c_2d, ("ev", si - 2, dm))
                row = dmq * 512 + dm * 128
                c_ri.inc(sy.dma_start(xres[si % 2][dm][:],
                                      xT_d[row:row+128, th*1024:(th+1)*1024]), 16)
            r.mark(("res", si), c_ri)
            for k in range(220):
                m = si * 220 + k
                if m >= ABUF:
                    sip, kp = divmod(m - ABUF, 220)
                    r.wait_val(sy, c_2p, sip * KC + kp + 1)
                c_ai.inc(sy.dma_start(abuf[m % ABUF][:],
                                      acts_d[k*128:(k+1)*128, th*1024:(th+1)*1024]),
                         16)

    def f2_gp(gp):
        r.wait(gp, c_ao, "all_acts")
        for si, (dmq, th) in enumerate(SWEEPS):
            for k in range(KC):
                m = si * KC + k
                if m >= ABUF:
                    sip, kp = divmod(m - ABUF, KC)
                    r.wait_val(gp, c_2p, sip * KC + kp + 1)
                if si == 0:
                    r.wait_val(gp, c_wc, 16 * (k + 1))
                c_wi.inc(gp.dma_start(wbuf[m % ABUF][:],
                                      wbf_d[k*128:(k+1)*128, dmq*512:(dmq+1)*512]),
                         16)

    def f2_pe(t):
        for si, (dmq, th) in enumerate(SWEEPS):
            if si > 0:
                r.wait(t, c_2d, ("ev", si - 1, 3))
            for k in range(KC):
                r.wait_val(t, c_wi, 16 * (si * KC + k + 1))
                if k < 220:
                    r.wait_val(t, c_ai, 16 * (si * 220 + k + 1))
                    mv = abuf[(si * 220 + k) % ABUF]
                    mv0, mv1 = mv[:, 0:512], mv[:, 512:1024]
                else:
                    qt0 = k - 220
                    mv0 = tmatT[:, qt0*S + th*1024: qt0*S + th*1024 + 512]
                    mv1 = tmatT[:, qt0*S + th*1024 + 512: qt0*S + (th+1)*1024]
                w = wbuf[(si * KC + k) % ABUF]
                last = None
                for dm in range(4):
                    wsl = w[:, dm*128:(dm+1)*128]
                    t.matmul(P[2*dm][:], wsl, mv0, start=(k == 0), stop=(k == KC - 1))
                    last = t.matmul(P[2*dm + 1][:], wsl, mv1,
                                    start=(k == 0), stop=(k == KC - 1))
                c_2p.inc(last)
            r.mark(("mm", si), c_2p)

    def f2_dve(v):
        for si, (dmq, th) in enumerate(SWEEPS):
            r.wait(v, c_2p, ("mm", si))
            r.wait(v, c_ri, ("res", si))
            for dm in range(4):
                if si >= 2:
                    r.wait(v, c_od, ("od", si - 2, dm))
                v.tensor_add(osb[si % 2][dm][:, 0:512], P[2*dm][:],
                             xres[si % 2][dm][:, 0:512])
                c_2d.inc(v.tensor_add(osb[si % 2][dm][:, 512:1024], P[2*dm + 1][:],
                                      xres[si % 2][dm][:, 512:1024]))
                r.mark(("ev", si, dm), c_2d)

    def f2_out(sc2):
        for si, (dmq, th) in enumerate(SWEEPS):
            for dm in range(4):
                r.wait(sc2, c_2d, ("ev", si, dm))
                row = dmq * 512 + dm * 128
                c_od.inc(sc2.dma_start(outT_d[row:row+128, th*1024:(th+1)*1024],
                                       osb[si % 2][dm][:]), 16)
                r.mark(("od", si, dm), c_od)
        sc2.wait_ge(c_od.sem, c_od.v)

    run_phase(nc, "f2", r, [("sync", f2_sync), ("gpsimd", f2_gp),
                            ("tensor", f2_pe), ("vector", f2_dve),
                            ("scalar", f2_out)],
              [c_ai, c_wi, c_ri, c_2p, c_2d, c_od])

    if dumps:
        _finish_dev(nc, dump_d, dumps, xn, tmatT, None, lam_sb, new_ctr)
    _close(sem_guards)
    return nc


def _finish_dev(nc, dump_d, dumps, xn, tmatT, combT_sb, lam_sb, new_ctr, sg_row=None):
    if not dump_d:
        return
    c_dd = new_ctr("dumpd")
    with nc.Block("dumps") as blk:
        @blk.sync
        def _(sy):
            if "xn" in dump_d:
                c_dd.inc(sy.dma_start(dump_d["xn"], xn[:]), 16)
            if "tmatT" in dump_d:
                c_dd.inc(sy.dma_start(dump_d["tmatT"], tmatT[:]), 16)
            if "combT" in dump_d and combT_sb is not None:
                c_dd.inc(sy.dma_start(dump_d["combT"], combT_sb[:]), 16)
            if "sg" in dump_d and sg_row is not None:
                c_dd.inc(sy.dma_start(dump_d["sg"], sg_row[:]), 16)
            if "lam" in dump_d:
                c_dd.inc(sy.dma_start(dump_d["lam"], lam_sb[:]), 16)
            sy.wait_ge(c_dd.sem, c_dd.v)


def _close(sem_guards):
    for g in reversed(sem_guards):
        g.__exit__(None, None, None)


def prep_inputs(inp):
    f32 = np.float32
    wq = np.asarray(inp["L0_attn_wq"], f32)
    wk = np.asarray(inp["L0_attn_wk"], f32)
    wv = np.asarray(inp["L0_attn_wv"], f32)
    wo = np.asarray(inp["L0_attn_wo"], f32)
    rw = np.asarray(inp["L0_ffn_router_w"], f32)
    sgw = np.asarray(inp["L0_ffn_sh_gate_w"], f32)
    w_down = np.asarray(inp["L0_ffn_w_down"], f32)
    sh_down = np.asarray(inp["L0_ffn_sh_down"], f32)
    import ml_dtypes
    common = {
        "wqT": np.ascontiguousarray(wq.T),
        "wkT": np.ascontiguousarray(wk.T),
        "wvT": np.ascontiguousarray(wv.T),
        "rw_l": np.ascontiguousarray(
            rw.T.reshape(KCH, 128, E).transpose(1, 0, 2).reshape(128, KCH * E)),
        "sgw_l": np.ascontiguousarray(sgw[:, 0].reshape(KCH, 128).T),
        "w_gate": np.asarray(inp["L0_ffn_w_gate"], f32),
        "w_up": np.asarray(inp["L0_ffn_w_up"], f32),
        "sh_gate": np.asarray(inp["L0_ffn_sh_gate"], f32),
        "sh_up": np.asarray(inp["L0_ffn_sh_up"], f32),
        "W_all": np.ascontiguousarray(
            np.concatenate([w_down.reshape(E * IM, D), sh_down, wo.T], axis=0)),
        "lq1": np.asarray(inp["L0_attn_lambda_q1"], f32).reshape(128, 1),
        "lq2": np.asarray(inp["L0_attn_lambda_q2"], f32).reshape(128, 1),
        "lk1": np.asarray(inp["L0_attn_lambda_k1"], f32).reshape(128, 1),
        "lk2": np.asarray(inp["L0_attn_lambda_k2"], f32).reshape(128, 1),
        "ones_c_bf": np.ones((128, 1), ml_dtypes.bfloat16),
        "ones_c_f": np.ones((128, 1), f32),
        "ones_r_f": np.ones((1, 128), f32),
        "ones_r_bf": np.ones((1, 128), ml_dtypes.bfloat16),
        "identf": np.eye(128, dtype=f32),
        "eps_ln": np.full((128, 1), LN_EPS, f32),
        "eps_rms": np.full((128, 1), RMS_EPS, f32),
    }
    x = np.asarray(inp["L0_attn_x"], f32)
    per_core = []
    for b in range(x.shape[0]):
        m = dict(common)
        m["xT"] = np.ascontiguousarray(x[b].T)
        per_core.append(m)
    return per_core


from concourse.bass_utils import run_bass_kernel_spmd


def kernel(**inputs):
    per_core = prep_inputs(inputs)
    nc = build_program(upto="f2", dumps=())
    res = run_bass_kernel_spmd(nc, per_core, core_ids=list(range(8)))
    out = np.stack([np.ascontiguousarray(res.results[b]["outT"].T)
                    for b in range(8)], axis=0)
    return out.astype(np.float32)


# version 05 (final)
# speedup vs baseline: 1.0226x; 1.0226x over previous
from contextlib import ExitStack

import numpy as np
import concourse.bass as bass
import concourse.mybir as mybir

F32 = mybir.dt.float32
BF16 = mybir.dt.bfloat16
AF = mybir.ActivationFunctionType
ALU = mybir.AluOpType
AX = mybir.AxisListType.X

D = 2048
S = 2048
NH = 16
KCH = 16
E = 16
IM = 1408
ITE = IM // 128
ISH = 5632
ITS = ISH // 128
N_ACT_TILES = E * ITE + ITS
ROWS_ACT = E * IM + ISH
ROWS_ALL = ROWS_ACT + D
KC = ROWS_ALL // 128
SCALE = float(1.0 / np.sqrt(np.float32(128)))
LAM0 = 0.8
RMS_EPS = 1.1920928955078125e-07
LN_EPS = 1e-5
ABUF = 6


class _FakeInstr:
    def then_inc(self, sem, n=1):
        return self


class _FakeEngine:
    def wait_ge(self, sem, v):
        pass

    def __getattr__(self, name):
        return lambda *a, **k: _FakeInstr()


class Ctr:
    def __init__(self, sem, name):
        self.sem = sem
        self.name = name
        self.v = 0

    def inc(self, instr, n=1):
        instr.then_inc(self.sem, n)
        self.v += n
        return self.v


class R:
    def __init__(self):
        self.ev = {}
        self.mode = "dry"

    def mark(self, key, ctr):
        if self.mode == "dry":
            self.ev[key] = ctr.v
        else:
            assert self.ev[key] == ctr.v, (key, self.ev[key], ctr.v)

    def wait(self, eng, ctr, key):
        if self.mode == "real":
            eng.wait_ge(ctr.sem, self.ev[key])

    def wait_val(self, eng, ctr, val):
        if self.mode == "real":
            eng.wait_ge(ctr.sem, val)


def run_phase(nc, name, r, streams, counters):
    snap = {c.name: c.v for c in counters}
    r.mode = "dry"
    fake = _FakeEngine()
    for _, fn in streams:
        fn(fake)
    dry_end = {}
    for c in counters:
        dry_end[c.name] = c.v
        c.v = snap[c.name]
    r.mode = "real"
    with nc.Block(name) as blk:
        for eng_name, fn in streams:
            getattr(blk, eng_name)(fn)
    for c in counters:
        assert c.v == dry_end[c.name], (c.name, c.v, dry_end[c.name])


def build_program(upto="f2", dumps=()):
    nc = bass.Bass("TRN2", target_bir_lowering=False)

    xT_d = nc.dram_tensor("xT", [D, S], F32, kind="ExternalInput").ap()
    wqT_d = nc.dram_tensor("wqT", [D, D], F32, kind="ExternalInput").ap()
    wkT_d = nc.dram_tensor("wkT", [D, D], F32, kind="ExternalInput").ap()
    wvT_d = nc.dram_tensor("wvT", [D, D], F32, kind="ExternalInput").ap()
    rw_d = nc.dram_tensor("rw_l", [128, KCH * E], F32, kind="ExternalInput").ap()
    sgw_d = nc.dram_tensor("sgw_l", [128, KCH], F32, kind="ExternalInput").ap()
    wg_d = nc.dram_tensor("w_gate", [E, D, IM], F32, kind="ExternalInput").ap()
    wu_d = nc.dram_tensor("w_up", [E, D, IM], F32, kind="ExternalInput").ap()
    shg_d = nc.dram_tensor("sh_gate", [D, ISH], F32, kind="ExternalInput").ap()
    shu_d = nc.dram_tensor("sh_up", [D, ISH], F32, kind="ExternalInput").ap()
    wall_d = nc.dram_tensor("W_all", [ROWS_ALL, D], F32, kind="ExternalInput").ap()
    lq1_d = nc.dram_tensor("lq1", [128, 1], F32, kind="ExternalInput").ap()
    lq2_d = nc.dram_tensor("lq2", [128, 1], F32, kind="ExternalInput").ap()
    lk1_d = nc.dram_tensor("lk1", [128, 1], F32, kind="ExternalInput").ap()
    lk2_d = nc.dram_tensor("lk2", [128, 1], F32, kind="ExternalInput").ap()
    onesc_bf_d = nc.dram_tensor("ones_c_bf", [128, 1], BF16, kind="ExternalInput").ap()
    onesc_f_d = nc.dram_tensor("ones_c_f", [128, 1], F32, kind="ExternalInput").ap()
    onesr_f_d = nc.dram_tensor("ones_r_f", [1, 128], F32, kind="ExternalInput").ap()
    onesr_bf_d = nc.dram_tensor("ones_r_bf", [1, 128], BF16, kind="ExternalInput").ap()
    identf_d = nc.dram_tensor("identf", [128, 128], F32, kind="ExternalInput").ap()
    epsln_d = nc.dram_tensor("eps_ln", [128, 1], F32, kind="ExternalInput").ap()
    epsrms_d = nc.dram_tensor("eps_rms", [128, 1], F32, kind="ExternalInput").ap()
    eps25_d = nc.dram_tensor("eps25", [128, 1], F32, kind="ExternalInput").ap()

    outT_d = nc.dram_tensor("outT", [D, S], F32, kind="ExternalOutput").ap()
    acts_kind = "ExternalOutput" if "acts" in dumps else "Internal"
    acts_d = nc.dram_tensor("acts", [ROWS_ACT, S], BF16, kind=acts_kind).ap()
    combT_d = nc.dram_tensor("combT_dram", [E, S], BF16, kind="Internal").ap()
    wbf_d = nc.dram_tensor("W_all_bf", [ROWS_ALL, D], BF16, kind="Internal").ap()

    dump_d = {}
    for nm, shape, dt in (("xn", [128, KCH * S], BF16),
                          ("tmatT", [128, KCH * S], BF16),
                          ("combT", [E, S], BF16),
                          ("sg", [1, S], F32),
                          ("lam", [1, 1], F32)):
        if nm in dumps:
            dump_d[nm] = nc.dram_tensor("dump_" + nm, shape, dt,
                                        kind="ExternalOutput").ap()

    PP = [nc.alloc_psum_tensor(f"PP{i}", [128, 1024], F32) for i in range(4)]
    P = []
    for i in range(4):
        P.append(PP[i][:, 0:512])
        P.append(PP[i][:, 512:1024])
    onesc_bf = nc.alloc_sbuf_tensor("onesc_bf", [128, 1], BF16)
    onesc_f = nc.alloc_sbuf_tensor("onesc_f", [128, 1], F32)
    onesr_f = nc.alloc_sbuf_tensor("onesr_f", [1, 128], F32)
    onesr_bf = nc.alloc_sbuf_tensor("onesr_bf", [1, 128], BF16)
    identf = nc.alloc_sbuf_tensor("identf_sb", [128, 128], F32)
    lam_sb = nc.alloc_sbuf_tensor("lam_sb", [1, 1], F32)
    lamn_sb = nc.alloc_sbuf_tensor("lamn_sb", [1, 1], F32)
    lamt_sb = nc.alloc_sbuf_tensor("lamt_sb", [1, 1], F32)
    epsln = nc.alloc_sbuf_tensor("epsln_sb", [128, 1], F32)
    epsrms = nc.alloc_sbuf_tensor("epsrms_sb", [128, 1], F32)
    eps25 = nc.alloc_sbuf_tensor("eps25_sb", [128, 1], F32)
    e1_sb = nc.alloc_sbuf_tensor("e1_sb", [1, 1], F32)
    e2_sb = nc.alloc_sbuf_tensor("e2_sb", [1, 1], F32)
    tmatT = nc.alloc_sbuf_tensor("tmatT", [128, KCH * S], BF16)

    sem_guards = []

    def new_ctr(name):
        g = nc.semaphore(name)
        sem_guards.append(g)
        return Ctr(g.__enter__(), name)

    r = R()
    sb_stack = ExitStack()

    def sb(name, shape, dtype):
        return sb_stack.enter_context(nc.sbuf_tensor(name, shape, dtype))

    xn_stack = ExitStack()
    xn = xn_stack.enter_context(nc.sbuf_tensor("xn", [128, KCH * S], BF16))
    qkv_stack = ExitStack()
    wq0s = qkv_stack.enter_context(nc.sbuf_tensor("wq0s", [128, S], BF16))
    c_aw = new_ctr("att_w")

    def _w_one(gp, wsrc, dstbuf, h):
        src = wsrc.rearrange("(c p) j -> p c j", p=128)[:, :, h*128:(h+1)*128]
        dst = dstbuf[:].rearrange("p (c j) -> p c j", j=128)
        c_aw.inc(gp.dma_start(dst[:, 0:8, :], src[:, 0:8, :]), 16)
        c_aw.inc(gp.dma_start(dst[:, 8:16, :], src[:, 8:16, :]), 16)

    lxq1 = sb("lxq1", [128, 1], F32)
    lxq2 = sb("lxq2", [128, 1], F32)
    lxk1 = sb("lxk1", [128, 1], F32)
    lxk2 = sb("lxk2", [128, 1], F32)
    xbuf = [sb(f"xbuf{i}", [128, S], F32) for i in range(2)]
    xsqb = [sb(f"xsqb{i}", [128, S], BF16) for i in range(2)]
    mean_row = sb("mean_row", [1, S], F32)
    mm_row = sb("mm_row", [1, S], F32)
    var_row = sb("var_row", [1, S], F32)
    std_row = sb("std_row", [1, S], F32)
    rstd_row = sb("rstd_row", [1, S], F32)
    lntmp = [sb(f"lntmp{i}", [128, 512], F32) for i in range(2)]

    c_ln_x = new_ctr("ln_x")
    c_ln_p = new_ctr("ln_p")
    c_ln_a = new_ctr("ln_a")
    c_ln_d = new_ctr("ln_d")

    def ln_gp(gp):
        _w_one(gp, wqT_d, wq0s, 0)
        r.mark("w0q", c_aw)

    def ln_sync(sy):
        for dst, srcd in ((onesc_bf, onesc_bf_d), (onesc_f, onesc_f_d),
                          (onesr_f, onesr_f_d), (onesr_bf, onesr_bf_d),
                          (identf, identf_d),
                          (epsln, epsln_d), (epsrms, epsrms_d), (eps25, eps25_d),
                          (lxq1, lq1_d), (lxq2, lq2_d),
                          (lxk1, lk1_d), (lxk2, lk2_d)):
            c_ln_x.inc(sy.dma_start(dst[:], srcd), 16)
        r.mark("consts", c_ln_x)
        for c in range(KCH):
            if c >= 2:
                r.wait(sy, c_ln_a, ("sqr", c - 2))
                r.wait(sy, c_ln_p, ("sums", c - 2))
            c_ln_x.inc(sy.dma_start(xbuf[c % 2][:], xT_d[c*128:(c+1)*128, :]), 16)
            r.mark(("xA", c), c_ln_x)
        for c in range(KCH):
            if c < 2:
                r.wait(sy, c_ln_a, ("sqr", KCH - 1))
                r.wait(sy, c_ln_p, ("sums", KCH - 1))
            else:
                r.wait(sy, c_ln_d, ("xn", c - 2))
            c_ln_x.inc(sy.dma_start(xbuf[c % 2][:], xT_d[c*128:(c+1)*128, :]), 16)
            r.mark(("xB", c), c_ln_x)

    def ln_pe(t):
        r.wait(t, c_ln_x, "consts")
        t.matmul(P[6][0:1, 0:1], lxq1[:], lxk1[:], start=True, stop=True)
        c_ln_p.inc(t.matmul(P[7][0:1, 0:1], lxq2[:], lxk2[:], start=True, stop=True))
        r.mark("lamdots", c_ln_p)
        r.wait(t, c_ln_a, "lamexp")
        for c in range(KCH):
            r.wait(t, c_ln_x, ("xA", c))
            for j in range(4):
                t.matmul(P[j][0:1, :], onesc_f[:], xbuf[c % 2][:, j*512:(j+1)*512],
                         start=(c == 0), stop=(c == KCH - 1))
            r.wait(t, c_ln_a, ("sqr", c))
            last = None
            for j in range(4):
                last = t.matmul(P[4 + j][0:1, :], onesc_bf[:],
                                xsqb[c % 2][:, j*512:(j+1)*512],
                                start=(c == 0), stop=(c == KCH - 1))
            c_ln_p.inc(last)
            r.mark(("sums", c), c_ln_p)
        r.wait(t, c_ln_d, "rstd")
        for j in range(4):
            t.matmul(P[j][:], onesr_f[:], mean_row[0:1, j*512:(j+1)*512],
                     start=True, stop=True)
        last = None
        for j in range(4):
            last = t.matmul(P[4 + j][:], onesr_f[:], rstd_row[0:1, j*512:(j+1)*512],
                            start=True, stop=True)
        c_ln_p.inc(last)
        r.mark("bcast", c_ln_p)

    def ln_act(sc):
        r.wait(sc, c_ln_p, "lamdots")
        sc.activation(e1_sb[:], P[6][0:1, 0:1], AF.Exp)
        c_ln_a.inc(sc.activation(e2_sb[:], P[7][0:1, 0:1], AF.Exp))
        r.mark("lamexp", c_ln_a)
        for c in range(KCH):
            r.wait(sc, c_ln_x, ("xA", c))
            c_ln_a.inc(sc.activation(xsqb[c % 2][:], xbuf[c % 2][:], AF.Square))
            r.mark(("sqr", c), c_ln_a)
        r.wait(sc, c_ln_d, "var")
        c_ln_a.inc(sc.activation(std_row[:], var_row[:], AF.Sqrt, bias=epsln[0:1, 0:1]))
        r.mark("std", c_ln_a)

    def ln_dve(v):
        def edge(ins):
            c_ln_d.inc(ins)
            r.wait_val(v, c_ln_d, c_ln_d.v)
        r.wait(v, c_ln_a, "lamexp")
        edge(v.tensor_sub(lamt_sb[:], e1_sb[:], e2_sb[:]))
        edge(v.tensor_single_scalar(lam_sb[:], lamt_sb[:], LAM0, ALU.add))
        c_ln_d.inc(v.tensor_single_scalar(lamn_sb[:], lam_sb[:], -1.0, ALU.mult))
        r.mark("lam", c_ln_d)
        r.wait(v, c_ln_p, ("sums", KCH - 1))
        last = None
        for j in range(4):
            last = v.tensor_single_scalar(mean_row[0:1, j*512:(j+1)*512],
                                          P[j][0:1, :], 1.0 / D, ALU.mult)
        edge(last)
        edge(v.tensor_mul(mm_row[:], mean_row[:], mean_row[:]))
        last = None
        for j in range(4):
            last = v.scalar_tensor_tensor(var_row[0:1, j*512:(j+1)*512],
                                          P[4 + j][0:1, :], 1.0 / D,
                                          mm_row[0:1, j*512:(j+1)*512],
                                          ALU.mult, ALU.subtract)
        c_ln_d.inc(last)
        r.mark("var", c_ln_d)
        r.wait(v, c_ln_a, "std")
        c_ln_d.inc(v.reciprocal(rstd_row[:], std_row[:]))
        r.mark("rstd", c_ln_d)
        r.wait(v, c_ln_p, "bcast")
        for c in range(KCH):
            r.wait(v, c_ln_x, ("xB", c))
            for j in range(4):
                edge(v.tensor_sub(lntmp[j % 2][:], xbuf[c % 2][:, j*512:(j+1)*512],
                                  P[j][:]))
                v.tensor_mul(xn[:, c*S + j*512: c*S + (j+1)*512], lntmp[j % 2][:],
                             P[4 + j][:])
            c_ln_d.inc(v.engine_nop())
            r.mark(("xn", c), c_ln_d)

    run_phase(nc, "ln", r, [("sync", ln_sync), ("gpsimd", ln_gp),
                            ("tensor", ln_pe), ("scalar", ln_act),
                            ("vector", ln_dve)],
              [c_ln_x, c_ln_p, c_ln_a, c_ln_d, c_aw])
    sb_stack.close()
    sb_stack = ExitStack()

    if upto == "ln":
        _finish_dev(nc, dump_d, dumps, xn, tmatT, None, lam_sb, new_ctr)
        _close(sem_guards)
        return nc

    qT = sb("qT", [128, S], BF16)
    kT = sb("kT", [128, S], BF16)
    vT = sb("vT", [128, S], F32)
    wqkv = [[sb(f"wqkv{i}_{j}", [128, S], BF16) for j in range(3)] for i in range(2)]
    vaug = sb("vaug", [128, KCH * 129], BF16)
    expb = [sb(f"expb{i}", [128, 1024], BF16) for i in range(2)]
    o_head = [sb(f"o_head{i}", [128, S], F32) for i in range(2)]
    ms_sb = [sb(f"ms_sb{i}", [128, 16], F32) for i in range(2)]
    rstd_sb = [sb(f"rstd_sb{i}", [128, 16], F32) for i in range(2)]
    lnm_sb = [sb(f"lnm_sb{i}", [128, 16], F32) for i in range(2)]
    sqd_sb = sb("sqd_sb", [128, 128], F32)
    t1_sb = sb("t1_sb", [128, 128], F32)
    t2_sb = sb("t2_sb", [128, 128], F32)
    r1_sb = sb("r1_sb", [128, 1], F32)
    r2_sb = sb("r2_sb", [128, 1], F32)
    r2l_sb = sb("r2l_sb", [128, 1], F32)
    lamn_bc = sb("lamn_bc", [128, 1], F32)


    c_prj = new_ctr("att_prj")
    c_pev = new_ctr("att_pev")
    c_sc = new_ctr("att_sc")
    c_exp = new_ctr("att_exp")
    c_av = new_ctr("att_av")
    c_cmb = new_ctr("att_cmb")
    c_sqa = new_ctr("att_sqa")
    c_hst = new_ctr("att_hst")

    def att_gp(gp):
        _w_one(gp, wkT_d, wqkv[0][1], 0)
        _w_one(gp, wvT_d, wqkv[0][2], 0)
        r.mark("wkv0", c_aw)
        for j, wsrc in enumerate((wqT_d, wkT_d, wvT_d)):
            _w_one(gp, wsrc, wqkv[1][j], 1)
        r.mark(("w", 1), c_aw)
        for h in range(2, NH):
            r.wait(gp, c_prj, ("proj", h - 2, 2))
            for j, wsrc in enumerate((wqT_d, wkT_d, wvT_d)):
                _w_one(gp, wsrc, wqkv[h % 2][j], h)
            r.mark(("w", h), c_aw)

    def _att_av(t, i2g, ktp):
        par = ktp % 2
        r.wait(t, c_exp, ("exp", i2g, ktp))
        if ktp == 0:
            r.wait(t, c_pev, ("vaug", i2g // 8, 1))
            if i2g > 0:
                r.wait(t, c_cmb, ("cmb", i2g - 1))
        last = None
        for sub in range(2):
            kt = ktp * 2 + sub
            mv = vaug[:, kt*129:(kt+1)*129]
            for lam_i in range(2):
                for qs in range(2):
                    base = lam_i*512 + sub*256 + qs*128
                    last = t.matmul(PP[2 + lam_i][:, qs*512: qs*512 + 129],
                                    expb[par][:, base: base + 128], mv,
                                    start=(ktp == 0 and sub == 0),
                                    stop=(ktp == 7 and sub == 1))
        c_av.inc(last)
        r.mark(("av", i2g, ktp), c_av)

    def att_pe(t):
        c_prj.inc(t.matmul(PP[2][:, 0:1], onesr_f[:], lamn_sb[:],
                           start=True, stop=True))
        r.mark("lambc", c_prj)
        for h in range(NH):
            r.wait(t, c_aw, "w0q" if h == 0 else ("w", h))
            if h > 0:
                r.wait(t, c_exp, ("exp", h * 8 - 1, 7))
            wq_src = wq0s if h == 0 else wqkv[h % 2][0]
            last = None
            for k in range(KCH):
                wsl = wq_src[:, k*128:(k+1)*128]
                for j in range(4):
                    last = t.matmul(P[j][:], wsl,
                                    xn[:, k*S + j*512: k*S + (j+1)*512],
                                    start=(k == 0), stop=(k == KCH - 1))
            c_prj.inc(last)
            r.mark(("proj", h, 0), c_prj)
            if h == 0:
                r.wait(t, c_aw, "wkv0")
                r.wait(t, c_pev, "lambc_ev")
            if h > 0:
                r.wait(t, c_cmb, ("cmb", h * 8 - 1))
            last = None
            for k in range(KCH):
                wsl = wqkv[h % 2][1][:, k*128:(k+1)*128]
                for j in range(4):
                    last = t.matmul(P[4 + j][:], wsl,
                                    xn[:, k*S + j*512: k*S + (j+1)*512],
                                    start=(k == 0), stop=(k == KCH - 1))
            c_prj.inc(last)
            r.mark(("proj", h, 1), c_prj)
            r.wait(t, c_pev, ("evq", h))
            last = None
            for k in range(KCH):
                wsl = wqkv[h % 2][2][:, k*128:(k+1)*128]
                for j in range(4):
                    last = t.matmul(P[j][:], wsl,
                                    xn[:, k*S + j*512: k*S + (j+1)*512],
                                    start=(k == 0), stop=(k == KCH - 1))
            c_prj.inc(last)
            r.mark(("proj", h, 2), c_prj)
            r.wait(t, c_pev, ("evk", h))
            r.wait(t, c_pev, ("evv", h))
            for g in range(2):
                last = None
                for kt in range(8):
                    ktg = g * 8 + kt
                    dstp = PP[2 + g][:, kt*128:(kt+1)*128]
                    last = t.transpose(dstp, vT[:, ktg*128:(ktg+1)*128], identf[:])
                c_prj.inc(last)
                r.mark(("vt", h, g), c_prj)
            for i2 in range(8):
                i2g = h * 8 + i2
                for ktp in range(8):
                    par = ktp % 2
                    slot = PP[par]
                    Pg = i2g * 8 + ktp
                    if Pg >= 2:
                        pi, pk = divmod(Pg - 2, 8)
                        r.wait(t, c_exp, ("exp", pi, pk))
                    last = None
                    for sub in range(2):
                        kt = ktp * 2 + sub
                        t.matmul(slot[:, sub*256: sub*256 + 256],
                                 kT[0:64, kt*128:(kt+1)*128],
                                 qT[0:64, i2*256:(i2+1)*256],
                                 start=True, stop=True, tile_position=(0, 0))
                        last = t.matmul(slot[:, 512 + sub*256: 512 + sub*256 + 256],
                                        kT[64:128, kt*128:(kt+1)*128],
                                        qT[64:128, i2*256:(i2+1)*256],
                                        start=True, stop=True,
                                        tile_position=(64, 0))
                    c_sc.inc(last)
                    r.mark(("sc", i2g, ktp), c_sc)
                    if ktp > 0:
                        _att_av(t, i2g, ktp - 1)
                _att_av(t, i2g, 7)

    def att_act(sc):
        for h in range(NH):
            for i2 in range(8):
                i2g = h * 8 + i2
                for ktp in range(8):
                    par = ktp % 2
                    r.wait(sc, c_sc, ("sc", i2g, ktp))
                    Pg = i2g * 8 + ktp
                    if Pg >= 2:
                        pi, pk = divmod(Pg - 2, 8)
                        r.wait(sc, c_av, ("av", pi, pk))
                    c_exp.inc(sc.activation(expb[par][:], PP[par][:, 0:1024], AF.Exp,
                                            scale=SCALE))
                    r.mark(("exp", i2g, ktp), c_exp)
                r.wait(sc, c_cmb, ("cmb", i2g))
                last = None
                for qs in range(2):
                    cix = i2 * 2 + qs
                    last = sc.activation(sqd_sb[:],
                                         o_head[h % 2][:, cix*128:(cix+1)*128],
                                         AF.Square,
                                         accum_out=ms_sb[h % 2][:, cix:cix+1])
                c_sqa.inc(last)
                r.mark(("sq", i2g), c_sqa)
            if h >= 2:
                r.wait(sc, c_hst, ("hst", h - 2))
            r.wait_val(sc, c_sqa, c_sqa.v)
            c_sqa.inc(sc.activation(rstd_sb[h % 2][:], ms_sb[h % 2][:], AF.Sqrt,
                                    scale=25.0 / 128, bias=eps25[:]))
            r.mark(("rstd", h), c_sqa)

    def _att_stores(v, h):
        r.wait(v, c_sqa, ("rstd", h))
        c_hst.inc(v.reciprocal(lnm_sb[h % 2][:], rstd_sb[h % 2][:]))
        r.wait_val(v, c_hst, c_hst.v)
        last = None
        for t_ in range(16):
            dst = bass.AP(tmatT, t_ * S + h, [[KCH * S, 128], [16, 128]])
            last = v.tensor_scalar(dst, o_head[h % 2][:, t_*128:(t_+1)*128],
                                   lnm_sb[h % 2][:, t_:t_+1], None, ALU.mult)
        c_hst.inc(last)
        r.mark(("hst", h), c_hst)

    def att_dve(v):
        r.wait(v, c_prj, "lambc")
        c_pev.inc(v.tensor_copy(lamn_bc[:], PP[2][:, 0:1]))
        r.mark("lambc_ev", c_pev)
        c_pev.inc(v.memset(vaug[:], 1.0))
        r.wait_val(v, c_pev, c_pev.v)
        for h in range(NH):
            if h >= 1:
                _att_stores(v, h - 1)
            r.wait(v, c_prj, ("proj", h, 0))
            if h > 0:
                r.wait(v, c_sc, ("sc", h * 8 - 1, 7))
            v.tensor_copy(qT[:, 0:1024], PP[0][:, 0:1024])
            c_pev.inc(v.tensor_copy(qT[:, 1024:2048], PP[1][:, 0:1024]))
            r.mark(("evq", h), c_pev)
            r.wait(v, c_prj, ("proj", h, 1))
            v.tensor_copy(kT[:, 0:1024], PP[2][:, 0:1024])
            c_pev.inc(v.tensor_copy(kT[:, 1024:2048], PP[3][:, 0:1024]))
            r.mark(("evk", h), c_pev)
            r.wait(v, c_prj, ("proj", h, 2))
            v.tensor_copy(vT[:, 0:1024], PP[0][:, 0:1024])
            c_pev.inc(v.tensor_copy(vT[:, 1024:2048], PP[1][:, 0:1024]))
            r.mark(("evv", h), c_pev)
            for g in range(2):
                r.wait(v, c_prj, ("vt", h, g))
                last = None
                for kt in range(8):
                    ktg = g * 8 + kt
                    last = v.tensor_copy(vaug[:, ktg*129: ktg*129 + 128],
                                         PP[2 + g][:, kt*128:(kt+1)*128])
                c_pev.inc(last)
                r.mark(("vaug", h, g), c_pev)
            for i2 in range(8):
                i2g = h * 8 + i2
                r.wait(v, c_av, ("av", i2g, 7))
                if i2 == 0 and h >= 2:
                    r.wait(v, c_hst, ("hst", h - 2))
                for qs in range(2):
                    O1 = PP[2][:, qs*512: qs*512 + 128]
                    d1 = PP[2][:, qs*512 + 128: qs*512 + 129]
                    O2 = PP[3][:, qs*512: qs*512 + 128]
                    d2 = PP[3][:, qs*512 + 128: qs*512 + 129]
                    v.reciprocal(r1_sb[:], d1)
                    c_cmb.inc(v.reciprocal(r2_sb[:], d2))
                    r.wait_val(v, c_cmb, c_cmb.v)
                    v.tensor_single_scalar(r2l_sb[:], r2_sb[:], lamn_bc[:], ALU.mult)
                    c_cmb.inc(v.tensor_scalar(t1_sb[:], O1, r1_sb[:], None, ALU.mult))
                    r.wait_val(v, c_cmb, c_cmb.v)
                    c_cmb.inc(v.tensor_scalar(t2_sb[:], O2, r2l_sb[:], None,
                                              ALU.mult))
                    r.wait_val(v, c_cmb, c_cmb.v)
                    cix = i2 * 2 + qs
                    c_cmb.inc(v.tensor_add(o_head[h % 2][:, cix*128:(cix+1)*128],
                                           t1_sb[:], t2_sb[:]))
                    r.wait_val(v, c_cmb, c_cmb.v)
                r.mark(("cmb", i2g), c_cmb)
        _att_stores(v, NH - 1)

    run_phase(nc, "att", r, [("gpsimd", att_gp), ("tensor", att_pe),
                             ("scalar", att_act), ("vector", att_dve)],
              [c_aw, c_prj, c_pev, c_sc, c_exp, c_av, c_cmb, c_sqa, c_hst])
    sb_stack.close()
    sb_stack = ExitStack()
    qkv_stack.close()

    if upto == "att":
        _finish_dev(nc, dump_d, dumps, xn, tmatT, None, lam_sb, new_ctr)
        _close(sem_guards)
        return nc

    rw_sb = sb("rw_sb", [128, KCH * E], BF16)
    sgw_sb = sb("sgw_sb", [128, KCH], BF16)
    p_sb = [sb(f"p_sb{i}", [128, E], F32) for i in range(2)]
    pw_sb = [sb(f"pw_sb{i}", [128, E], F32) for i in range(2)]
    nmax_sb = sb("nmax_sb", [128, 1], F32)
    psum_sb = sb("psum_sb", [128, 1], F32)
    rs_sb = sb("rs_sb", [128, 1], F32)
    mx_sb = sb("mx_sb", [128, 1], F32)
    mask_sb = sb("mask_sb", [128, E], F32)
    combv_sb = sb("combv_sb", [128, E], F32)
    ts_sb = sb("ts_sb", [128, 1], F32)
    ts2_sb = sb("ts2_sb", [128, 1], F32)
    probs_sb = sb("probs_sb", [128, E], F32)
    comb_sb = [sb(f"comb_sb{i}", [128, E], F32) for i in range(2)]
    combT_sb = sb("combT_sb", [E, S], BF16)
    crow_sb = [sb(f"crow_sb{i}", [1, S], BF16) for i in range(2)]
    sg_row = sb("sg_row", [1, S], F32)
    wgub = [[sb(f"wgub{i}_{j}", [128, KCH * 128], BF16) for j in range(2)]
            for i in range(2)]
    shwb = [[sb(f"shwb{i}_{j}", [128, KCH * 128], BF16) for j in range(2)]
            for i in range(2)]
    silu_sb = [sb(f"silu_sb{i}", [128, 1024], BF16) for i in range(2)]
    tmp_sb = [sb(f"tmp_sb{i}", [128, 1024], BF16) for i in range(2)]
    act_sb = [sb(f"act_sb{i}", [128, S], BF16) for i in range(2)]
    cb_sb = sb("cb_sb", [128, S], BF16)
    sgb_sb = sb("sgb_sb", [128, S], BF16)

    c_fw = new_ctr("f1_w")
    c_fp = new_ctr("f1_pe")
    c_fa = new_ctr("f1_act")
    c_fd = new_ctr("f1_dve")
    c_ao = new_ctr("f1_aout")
    c_ct = new_ctr("f1_combT")
    c_cr = new_ctr("f1_crow")
    c_wc = new_ctr("f1_wcast")

    NTT = 16

    def prev_tile(e, it):
        return (e, it - 1) if it > 0 else (e - 1, ITE - 1)

    def _crow(gp, e):
        if e == 0:
            r.wait(gp, c_ct, "bounce")
        if e >= 2:
            r.wait(gp, c_fp, ("cbq", e - 2))
        c_cr.inc(gp.dma_start(crow_sb[e % 2][:], combT_d[e:e+1, :]), 16)
        r.mark(("crow", e), c_cr)

    def f1_gp(gp):
        c_fw.inc(gp.dma_start(rw_sb[:], rw_d), 16)
        c_fw.inc(gp.dma_start(sgw_sb[:], sgw_d), 16)
        r.mark("rw", c_fw)
        _crow(gp, 0)
        _crow(gp, 1)
        for e in range(E):
            for it in range(ITE):
                m = e * ITE + it
                if m >= 2:
                    ep, ip = divmod(m - 2, ITE)
                    r.wait(gp, c_fp, ("gup", ep, ip, 1))
                for j, wsrc in enumerate((wg_d, wu_d)):
                    src = wsrc[e].rearrange("(c p) i -> p c i",
                                            p=128)[:, :, it*128:(it+1)*128]
                    dst = wgub[m % 2][j][:].rearrange("p (c i) -> p c i", i=128)
                    c_fw.inc(gp.dma_start(dst[:, 0:8, :], src[:, 0:8, :]), 16)
                    c_fw.inc(gp.dma_start(dst[:, 8:16, :], src[:, 8:16, :]), 16)
                r.mark(("wet", e, it), c_fw)
            if e + 2 < E:
                _crow(gp, e + 2)
            for k in range(e * 15, min((e + 1) * 15, KC)):
                c_wc.inc(gp.dma_start(wbf_d[k*128:(k+1)*128, :],
                                      wall_d[k*128:(k+1)*128, :]), 16)
        for it in range(ITS):
            if it >= 2:
                r.wait(gp, c_fp, ("gup_sh", it - 2, 1))
            for j, wsrc in enumerate((shg_d, shu_d)):
                src = wsrc.rearrange("(c p) i -> p c i", p=128)[:, :, it*128:(it+1)*128]
                dst = shwb[it % 2][j][:].rearrange("p (c i) -> p c i", i=128)
                c_fw.inc(gp.dma_start(dst, src), 16)
            r.mark(("wsh", it), c_fw)

    def f1_pe(t):
        r.wait(t, c_fw, "rw")
        last = None
        for k in range(KCH):
            for j in range(4):
                last = t.matmul(P[4 + j][0:1, :], sgw_sb[:, k:k+1],
                                xn[:, k*S + j*512: k*S + (j+1)*512],
                                start=(k == 0), stop=(k == KCH - 1))
        c_fp.inc(last)
        r.mark("sglog", c_fp)
        for tt in range(NTT):
            if tt >= 2:
                r.wait(t, c_fd, ("nmax", tt - 2))
            last = None
            for k in range(KCH):
                last = t.matmul(P[tt % 2][:, 0:E],
                                xn[:, k*S + tt*128: k*S + (tt+1)*128],
                                rw_sb[:, k*E:(k+1)*E],
                                start=(k == 0), stop=(k == KCH - 1))
            c_fp.inc(last)
            r.mark(("rlog", tt), c_fp)
            if tt >= 2:
                r.wait(t, c_fa, ("combT", tt - 2))
            r.wait(t, c_fd, ("comb", tt))
            c_fp.inc(t.transpose(P[2 + tt % 2][0:E, 0:128], comb_sb[tt % 2][:],
                                 identf[:]))
            r.mark(("combtr", tt), c_fp)
        r.wait(t, c_fa, "sgrow")
        last = None
        for j in range(4):
            last = t.matmul(P[4 + j][:], onesr_f[:], sg_row[0:1, j*512:(j+1)*512],
                            start=True, stop=True)
        c_fp.inc(last)
        r.mark("sgb", c_fp)
        for e in range(E):
            r.wait(t, c_cr, ("crow", e))
            if e == 0:
                r.wait(t, c_fa, "sgbe")
                r.wait(t, c_fa, ("combT", NTT - 1))
                r.wait(t, c_fd, ("nmax", NTT - 1))
            else:
                r.wait(t, c_fa, ("cbe", e - 1))
                r.wait(t, c_fa, ("silu", e - 1, ITE - 1, 0))
                r.wait(t, c_fd, ("actq", e - 1, ITE - 1, 1))
            last = None
            for j in range(4):
                last = t.matmul(P[j][:], onesr_bf[:],
                                crow_sb[e % 2][0:1, j*512:(j+1)*512],
                                start=True, stop=True)
            c_fp.inc(last)
            r.mark(("cbq", e), c_fp)
            r.wait(t, c_fa, ("cbe", e))
            for it in range(ITE):
                m = e * ITE + it
                r.wait(t, c_fw, ("wet", e, it))
                for th in range(2):
                    gA, gB = P[4*th], P[4*th + 1]
                    uA, uB = P[4*th + 2], P[4*th + 3]
                    if not (e == 0 and it == 0):
                        pe_, pit = prev_tile(e, it)
                        r.wait(t, c_fa, ("silu", pe_, pit, th))
                        r.wait(t, c_fd, ("umul", pe_, pit, th))
                    last = None
                    for k in range(KCH):
                        wsl = wgub[m % 2][0][:, k*128:(k+1)*128]
                        t.matmul(gA[:], wsl, xn[:, k*S + th*1024: k*S + th*1024 + 512],
                                 start=(k == 0), stop=(k == KCH - 1))
                        last = t.matmul(gB[:], wsl,
                                        xn[:, k*S + th*1024 + 512: k*S + (th+1)*1024],
                                        start=(k == 0), stop=(k == KCH - 1))
                    c_fp.inc(last)
                    r.mark(("g", e, it, th), c_fp)
                    last = None
                    for k in range(KCH):
                        wsl = wgub[m % 2][1][:, k*128:(k+1)*128]
                        t.matmul(uA[:], wsl, xn[:, k*S + th*1024: k*S + th*1024 + 512],
                                 start=(k == 0), stop=(k == KCH - 1))
                        last = t.matmul(uB[:], wsl,
                                        xn[:, k*S + th*1024 + 512: k*S + (th+1)*1024],
                                        start=(k == 0), stop=(k == KCH - 1))
                    c_fp.inc(last)
                    r.mark(("gup", e, it, th), c_fp)
        for it in range(ITS):
            r.wait(t, c_fw, ("wsh", it))
            for th in range(2):
                gA, gB = P[4*th], P[4*th + 1]
                uA, uB = P[4*th + 2], P[4*th + 3]
                if it == 0:
                    r.wait(t, c_fa, ("silu", E - 1, ITE - 1, th))
                    r.wait(t, c_fd, ("umul", E - 1, ITE - 1, th))
                else:
                    r.wait(t, c_fa, ("silu_sh", it - 1, th))
                    r.wait(t, c_fd, ("umul_sh", it - 1, th))
                last = None
                for k in range(KCH):
                    wsl = shwb[it % 2][0][:, k*128:(k+1)*128]
                    t.matmul(gA[:], wsl, xn[:, k*S + th*1024: k*S + th*1024 + 512],
                             start=(k == 0), stop=(k == KCH - 1))
                    last = t.matmul(gB[:], wsl,
                                    xn[:, k*S + th*1024 + 512: k*S + (th+1)*1024],
                                    start=(k == 0), stop=(k == KCH - 1))
                c_fp.inc(last)
                r.mark(("g_sh", it, th), c_fp)
                last = None
                for k in range(KCH):
                    wsl = shwb[it % 2][1][:, k*128:(k+1)*128]
                    t.matmul(uA[:], wsl, xn[:, k*S + th*1024: k*S + th*1024 + 512],
                             start=(k == 0), stop=(k == KCH - 1))
                    last = t.matmul(uB[:], wsl,
                                    xn[:, k*S + th*1024 + 512: k*S + (th+1)*1024],
                                    start=(k == 0), stop=(k == KCH - 1))
                c_fp.inc(last)
                r.mark(("gup_sh", it, th), c_fp)

    def f1_act(sc):
        r.wait(sc, c_fp, "sglog")
        last = None
        for j in range(4):
            last = sc.activation(sg_row[0:1, j*512:(j+1)*512], P[4 + j][0:1, :],
                                 AF.Sigmoid)
        c_fa.inc(last)
        r.mark("sgrow", c_fa)
        for tt in range(NTT):
            r.wait(sc, c_fp, ("rlog", tt))
            r.wait(sc, c_fd, ("nmax", tt))
            c_fa.inc(sc.activation(p_sb[tt % 2][:], P[tt % 2][:, 0:E], AF.Exp,
                                   bias=nmax_sb[:], accum_out=psum_sb[:]))
            r.mark(("rexp", tt), c_fa)
            r.wait(sc, c_fp, ("combtr", tt))
            c_fa.inc(sc.copy(combT_sb[:, tt*128:(tt+1)*128],
                             P[2 + tt % 2][0:E, 0:128]))
            r.mark(("combT", tt), c_fa)
        r.wait(sc, c_fp, "sgb")
        last = None
        for j in range(4):
            last = sc.copy(sgb_sb[:, j*512:(j+1)*512], P[4 + j][:])
        c_fa.inc(last)
        r.mark("sgbe", c_fa)
        for e in range(E):
            r.wait(sc, c_fp, ("cbq", e))
            last = None
            for j in range(4):
                last = sc.copy(cb_sb[:, j*512:(j+1)*512], P[j][:])
            c_fa.inc(last)
            r.mark(("cbe", e), c_fa)
            for it in range(ITE):
                for th in range(2):
                    gA, gB = P[4*th], P[4*th + 1]
                    r.wait(sc, c_fp, ("g", e, it, th))
                    if not (e == 0 and it == 0):
                        pe_, pit = prev_tile(e, it)
                        r.wait(sc, c_fd, ("umul", pe_, pit, th))
                    sc.activation(silu_sb[th][:, 0:512], gA[:], AF.Silu)
                    c_fa.inc(sc.activation(silu_sb[th][:, 512:1024], gB[:], AF.Silu))
                    r.mark(("silu", e, it, th), c_fa)
        for it in range(ITS):
            for th in range(2):
                gA, gB = P[4*th], P[4*th + 1]
                r.wait(sc, c_fp, ("g_sh", it, th))
                if it == 0:
                    r.wait(sc, c_fd, ("umul", E - 1, ITE - 1, th))
                else:
                    r.wait(sc, c_fd, ("umul_sh", it - 1, th))
                sc.activation(silu_sb[th][:, 0:512], gA[:], AF.Silu)
                c_fa.inc(sc.activation(silu_sb[th][:, 512:1024], gB[:], AF.Silu))
                r.mark(("silu_sh", it, th), c_fa)

    def f1_dve(v):
        for tt in range(NTT):
            r.wait(v, c_fp, ("rlog", tt))
            c_fd.inc(v.reduce_max(nmax_sb[:], P[tt % 2][:, 0:E], axis=AX, negate=True))
            r.mark(("nmax", tt), c_fd)
            r.wait(v, c_fa, ("rexp", tt))
            def edge(ins):
                c_fd.inc(ins)
                r.wait_val(v, c_fd, c_fd.v)
            edge(v.reciprocal(rs_sb[:], psum_sb[:]))
            edge(v.tensor_scalar(probs_sb[:], p_sb[tt % 2][:], rs_sb[:], None,
                                 ALU.mult))
            edge(v.tensor_copy(pw_sb[0][:], probs_sb[:]))
            for rnd in range(4):
                edge(v.reduce_max(mx_sb[:], pw_sb[rnd % 2][:], axis=AX))
                if rnd < 3:
                    edge(v.tensor_single_scalar(mask_sb[:], pw_sb[rnd % 2][:],
                                                mx_sb[:], ALU.is_ge))
                    edge(v.scalar_tensor_tensor(pw_sb[(rnd + 1) % 2][:], mask_sb[:],
                                                -1e30, pw_sb[rnd % 2][:],
                                                ALU.mult, ALU.add))
            edge(v.tensor_single_scalar(mask_sb[:], probs_sb[:], mx_sb[:], ALU.is_ge))
            edge(v.tensor_mul(combv_sb[:], probs_sb[:], mask_sb[:]))
            edge(v.reduce_sum(ts_sb[:], combv_sb[:], axis=AX))
            edge(v.tensor_scalar_max(ts2_sb[:], ts_sb[:], 1e-9))
            edge(v.reciprocal(rs_sb[:], ts2_sb[:]))
            if tt >= 2:
                r.wait(v, c_fp, ("combtr", tt - 2))
            c_fd.inc(v.tensor_scalar(comb_sb[tt % 2][:], combv_sb[:], rs_sb[:], None,
                                     ALU.mult))
            r.mark(("comb", tt), c_fd)
        for e in range(E):
            for it in range(ITE):
                n = e * ITE + it
                for th in range(2):
                    uA, uB = P[4*th + 2], P[4*th + 3]
                    r.wait(v, c_fa, ("silu", e, it, th))
                    r.wait(v, c_fp, ("gup", e, it, th))
                    if th == 0 and n >= 2:
                        r.wait_val(v, c_ao, 16 * (n - 1))
                    v.tensor_mul(tmp_sb[th][:, 0:512], silu_sb[th][:, 0:512], uA[:])
                    c_fd.inc(v.tensor_mul(tmp_sb[th][:, 512:1024],
                                          silu_sb[th][:, 512:1024], uB[:]))
                    r.mark(("umul", e, it, th), c_fd)
                    r.wait_val(v, c_fd, c_fd.v)
                    c_fd.inc(v.tensor_mul(act_sb[n % 2][:, th*1024:(th+1)*1024],
                                          tmp_sb[th][:],
                                          cb_sb[:, th*1024:(th+1)*1024]))
                    r.mark(("actq", e, it, th), c_fd)
                r.mark(("acttile", n), c_fd)
        for it in range(ITS):
            n = E * ITE + it
            for th in range(2):
                uA, uB = P[4*th + 2], P[4*th + 3]
                r.wait(v, c_fa, ("silu_sh", it, th))
                r.wait(v, c_fp, ("gup_sh", it, th))
                if th == 0 and n >= 2:
                    r.wait_val(v, c_ao, 16 * (n - 1))
                v.tensor_mul(tmp_sb[th][:, 0:512], silu_sb[th][:, 0:512], uA[:])
                c_fd.inc(v.tensor_mul(tmp_sb[th][:, 512:1024],
                                      silu_sb[th][:, 512:1024], uB[:]))
                r.mark(("umul_sh", it, th), c_fd)
                r.wait_val(v, c_fd, c_fd.v)
                c_fd.inc(v.tensor_mul(act_sb[n % 2][:, th*1024:(th+1)*1024],
                                      tmp_sb[th][:], sgb_sb[:, th*1024:(th+1)*1024]))
                r.mark(("actq_sh", it, th), c_fd)
            r.mark(("acttile", n), c_fd)

    def f1_sync(sy):
        r.wait(sy, c_fa, ("combT", NTT - 1))
        c_ct.inc(sy.dma_start(combT_d, combT_sb[:]), 16)
        r.mark("bounce", c_ct)
        for e in range(E):
            for it in range(ITE):
                n = e * ITE + it
                r.wait(sy, c_fd, ("acttile", n))
                row = e * IM + it * 128
                c_ao.inc(sy.dma_start(acts_d[row:row+128, :], act_sb[n % 2][:]), 16)
        for it in range(ITS):
            n = E * ITE + it
            r.wait(sy, c_fd, ("acttile", n))
            row = E * IM + it * 128
            c_ao.inc(sy.dma_start(acts_d[row:row+128, :], act_sb[n % 2][:]), 16)
        r.mark("all_acts", c_ao)

    run_phase(nc, "f1", r, [("gpsimd", f1_gp), ("tensor", f1_pe),
                            ("scalar", f1_act), ("vector", f1_dve),
                            ("sync", f1_sync)],
              [c_fw, c_fp, c_fa, c_fd, c_ao, c_ct, c_cr, c_wc])
    if upto == "f1":
        _finish_dev(nc, dump_d, dumps, xn, tmatT, combT_sb, lam_sb, new_ctr,
                    sg_row=sg_row)
        _close(sem_guards)
        return nc
    sb_stack.close()
    sb_stack = ExitStack()
    xn_stack.close()

    abuf = [sb(f"abuf{i}", [128, 1024], BF16) for i in range(ABUF)]
    wbuf = [sb(f"wbuf{i}", [128, 512], BF16) for i in range(ABUF)]
    xres = [[sb(f"xres{i}_{j}", [128, 1024], F32) for j in range(4)] for i in range(2)]
    osb = [[sb(f"osb{i}_{j}", [128, 1024], F32) for j in range(4)] for i in range(2)]

    c_ai = new_ctr("f2_ain")
    c_wi = new_ctr("f2_win")
    c_ri = new_ctr("f2_res")
    c_2p = new_ctr("f2_pe")
    c_2d = new_ctr("f2_dve")
    c_od = new_ctr("f2_out")

    SWEEPS = [(dmq, th) for dmq in range(4) for th in range(2)]

    def f2_sync(sy):
        r.wait(sy, c_ao, "all_acts")
        for si, (dmq, th) in enumerate(SWEEPS):
            for dm in range(4):
                if si >= 2:
                    r.wait(sy, c_2d, ("ev", si - 2, dm))
                row = dmq * 512 + dm * 128
                c_ri.inc(sy.dma_start(xres[si % 2][dm][:],
                                      xT_d[row:row+128, th*1024:(th+1)*1024]), 16)
            r.mark(("res", si), c_ri)
            for k in range(220):
                m = si * 220 + k
                if m >= ABUF:
                    sip, kp = divmod(m - ABUF, 220)
                    r.wait_val(sy, c_2p, sip * KC + kp + 1)
                c_ai.inc(sy.dma_start(abuf[m % ABUF][:],
                                      acts_d[k*128:(k+1)*128, th*1024:(th+1)*1024]),
                         16)

    def f2_gp(gp):
        r.wait(gp, c_ao, "all_acts")
        for si, (dmq, th) in enumerate(SWEEPS):
            for k in range(KC):
                m = si * KC + k
                if m >= ABUF:
                    sip, kp = divmod(m - ABUF, KC)
                    r.wait_val(gp, c_2p, sip * KC + kp + 1)
                if si == 0:
                    r.wait_val(gp, c_wc, 16 * (k + 1))
                c_wi.inc(gp.dma_start(wbuf[m % ABUF][:],
                                      wbf_d[k*128:(k+1)*128, dmq*512:(dmq+1)*512]),
                         16)

    def f2_pe(t):
        for si, (dmq, th) in enumerate(SWEEPS):
            if si > 0:
                r.wait(t, c_2d, ("ev", si - 1, 3))
            for k in range(KC):
                r.wait_val(t, c_wi, 16 * (si * KC + k + 1))
                if k < 220:
                    r.wait_val(t, c_ai, 16 * (si * 220 + k + 1))
                    mv = abuf[(si * 220 + k) % ABUF]
                    mv0, mv1 = mv[:, 0:512], mv[:, 512:1024]
                else:
                    qt0 = k - 220
                    mv0 = tmatT[:, qt0*S + th*1024: qt0*S + th*1024 + 512]
                    mv1 = tmatT[:, qt0*S + th*1024 + 512: qt0*S + (th+1)*1024]
                w = wbuf[(si * KC + k) % ABUF]
                last = None
                for dm in range(4):
                    wsl = w[:, dm*128:(dm+1)*128]
                    t.matmul(P[2*dm][:], wsl, mv0, start=(k == 0), stop=(k == KC - 1))
                    last = t.matmul(P[2*dm + 1][:], wsl, mv1,
                                    start=(k == 0), stop=(k == KC - 1))
                c_2p.inc(last)
            r.mark(("mm", si), c_2p)

    def f2_dve(v):
        for si, (dmq, th) in enumerate(SWEEPS):
            r.wait(v, c_2p, ("mm", si))
            r.wait(v, c_ri, ("res", si))
            for dm in range(4):
                if si >= 2:
                    r.wait(v, c_od, ("od", si - 2, dm))
                v.tensor_add(osb[si % 2][dm][:, 0:512], P[2*dm][:],
                             xres[si % 2][dm][:, 0:512])
                c_2d.inc(v.tensor_add(osb[si % 2][dm][:, 512:1024], P[2*dm + 1][:],
                                      xres[si % 2][dm][:, 512:1024]))
                r.mark(("ev", si, dm), c_2d)

    def f2_out(sc2):
        for si, (dmq, th) in enumerate(SWEEPS):
            for dm in range(4):
                r.wait(sc2, c_2d, ("ev", si, dm))
                row = dmq * 512 + dm * 128
                c_od.inc(sc2.dma_start(outT_d[row:row+128, th*1024:(th+1)*1024],
                                       osb[si % 2][dm][:]), 16)
                r.mark(("od", si, dm), c_od)
        sc2.wait_ge(c_od.sem, c_od.v)

    run_phase(nc, "f2", r, [("sync", f2_sync), ("gpsimd", f2_gp),
                            ("tensor", f2_pe), ("vector", f2_dve),
                            ("scalar", f2_out)],
              [c_ai, c_wi, c_ri, c_2p, c_2d, c_od])

    if dumps:
        _finish_dev(nc, dump_d, dumps, xn, tmatT, None, lam_sb, new_ctr)
    _close(sem_guards)
    return nc


def _finish_dev(nc, dump_d, dumps, xn, tmatT, combT_sb, lam_sb, new_ctr, sg_row=None):
    if not dump_d:
        return
    c_dd = new_ctr("dumpd")
    with nc.Block("dumps") as blk:
        @blk.sync
        def _(sy):
            if "xn" in dump_d:
                c_dd.inc(sy.dma_start(dump_d["xn"], xn[:]), 16)
            if "tmatT" in dump_d:
                c_dd.inc(sy.dma_start(dump_d["tmatT"], tmatT[:]), 16)
            if "combT" in dump_d and combT_sb is not None:
                c_dd.inc(sy.dma_start(dump_d["combT"], combT_sb[:]), 16)
            if "sg" in dump_d and sg_row is not None:
                c_dd.inc(sy.dma_start(dump_d["sg"], sg_row[:]), 16)
            if "lam" in dump_d:
                c_dd.inc(sy.dma_start(dump_d["lam"], lam_sb[:]), 16)
            sy.wait_ge(c_dd.sem, c_dd.v)


def _close(sem_guards):
    for g in reversed(sem_guards):
        g.__exit__(None, None, None)


def prep_inputs(inp):
    f32 = np.float32
    wq = np.asarray(inp["L0_attn_wq"], f32)
    wk = np.asarray(inp["L0_attn_wk"], f32)
    wv = np.asarray(inp["L0_attn_wv"], f32)
    wo = np.asarray(inp["L0_attn_wo"], f32)
    rw = np.asarray(inp["L0_ffn_router_w"], f32)
    sgw = np.asarray(inp["L0_ffn_sh_gate_w"], f32)
    w_down = np.asarray(inp["L0_ffn_w_down"], f32)
    sh_down = np.asarray(inp["L0_ffn_sh_down"], f32)
    import ml_dtypes
    common = {
        "wqT": np.ascontiguousarray(wq.T),
        "wkT": np.ascontiguousarray(wk.T),
        "wvT": np.ascontiguousarray(wv.T),
        "rw_l": np.ascontiguousarray(
            rw.T.reshape(KCH, 128, E).transpose(1, 0, 2).reshape(128, KCH * E)),
        "sgw_l": np.ascontiguousarray(sgw[:, 0].reshape(KCH, 128).T),
        "w_gate": np.asarray(inp["L0_ffn_w_gate"], f32),
        "w_up": np.asarray(inp["L0_ffn_w_up"], f32),
        "sh_gate": np.asarray(inp["L0_ffn_sh_gate"], f32),
        "sh_up": np.asarray(inp["L0_ffn_sh_up"], f32),
        "W_all": np.ascontiguousarray(
            np.concatenate([w_down.reshape(E * IM, D), sh_down, wo.T], axis=0)),
        "lq1": np.asarray(inp["L0_attn_lambda_q1"], f32).reshape(128, 1),
        "lq2": np.asarray(inp["L0_attn_lambda_q2"], f32).reshape(128, 1),
        "lk1": np.asarray(inp["L0_attn_lambda_k1"], f32).reshape(128, 1),
        "lk2": np.asarray(inp["L0_attn_lambda_k2"], f32).reshape(128, 1),
        "ones_c_bf": np.ones((128, 1), ml_dtypes.bfloat16),
        "ones_c_f": np.ones((128, 1), f32),
        "ones_r_f": np.ones((1, 128), f32),
        "ones_r_bf": np.ones((1, 128), ml_dtypes.bfloat16),
        "identf": np.eye(128, dtype=f32),
        "eps_ln": np.full((128, 1), LN_EPS, f32),
        "eps_rms": np.full((128, 1), RMS_EPS, f32),
        "eps25": np.full((128, 1), 25.0 * RMS_EPS, f32),
    }
    x = np.asarray(inp["L0_attn_x"], f32)
    per_core = []
    for b in range(x.shape[0]):
        m = dict(common)
        m["xT"] = np.ascontiguousarray(x[b].T)
        per_core.append(m)
    return per_core


from concourse.bass_utils import run_bass_kernel_spmd


def kernel(**inputs):
    per_core = prep_inputs(inputs)
    nc = build_program(upto="f2", dumps=())
    res = run_bass_kernel_spmd(nc, per_core, core_ids=list(range(8)))
    out = np.stack([np.ascontiguousarray(res.results[b]["outT"].T)
                    for b in range(8)], axis=0)
    return out.astype(np.float32)
